# version 03
# baseline (speedup 1.0000x reference)
import contextlib
import os
import numpy as np
import ml_dtypes
import concourse.bass as bass
import concourse.mybir as mybir
from concourse.bass_utils import run_bass_kernel_spmd

F32 = mybir.dt.float32
BF16 = mybir.dt.bfloat16
AF = mybir.ActivationFunctionType
ALU = mybir.AluOpType
AX = mybir.AxisListType

B, S, D = 8, 2048, 2048
NF = D // 128
NTC = S // 128
HALF = S // 2
NKV, NH, HD = 4, 16, 128
FDIM, NFC = 768, 6
NEXP = 8
LN_EPS = 1e-5
RMS_EPS = 1e-5
NBUF = 4

bf16np = ml_dtypes.bfloat16

C_IP1, C_127M, C_128, C_N1E30, C_1E8, C_1E12, C_ONE, C_LNEPS, C_NEG1 = range(9)


def ps_flatten(ps):
    return ps.rearrange("p a b -> p (a b)")


def _shift_mats():
    mats = np.zeros((3, 2, 128, 128), dtype=np.float32)
    for j in (1, 2, 3):
        for m in range(128):
            if m - j >= 0:
                mats[j - 1, 0, m - j, m] = 1.0
            else:
                mats[j - 1, 1, 128 + m - j, m] = 1.0
    return mats


class Sem:
    def __init__(self, nc, stk, name):
        self.h = stk.enter_context(nc.semaphore(name))
        self.n = 0

    def inc(self, inst, k=1):
        inst.then_inc(self.h, k)
        self.n += k
        return self.n


class Eng:
    def __init__(self, eng, name, waited):
        self.e = eng
        self.name = name
        self.waited = waited

    def wait(self, sem, val):
        if val is None or val <= 0:
            return
        key = id(sem)
        if self.waited.get(key, 0) >= val:
            return
        self.e.wait_ge(sem.h, val)
        self.waited[key] = val

    def __getattr__(self, a):
        return getattr(self.e, a)


class K:

    def __init__(self, debug_out=()):
        nc = bass.Bass("TRN2", target_bir_lowering=False)
        self.nc = nc
        self.stk = contextlib.ExitStack()
        self.debug_out = set(debug_out)
        stk = self.stk

        dd = {}
        def di(name, shape, dt=F32):
            dd[name] = nc.dram_tensor(name, shape, dt, kind="ExternalInput").ap()
        di("x0", [D, S])
        di("aWqT", [D, D]); di("aWkT", [D, 512]); di("aWvT", [D, 512])
        di("aWgT", [D, D]); di("aWmbT", [D, D])
        di("abq", [128, NF]); di("abk", [128, NKV]); di("abv", [128, NKV])
        di("abg", [128, NF]); di("abmb", [128, NF])
        di("amem", [NH, FDIM, HD]); di("anormT", [128, NFC, NH])
        di("bld", [NH])
        di("bWq", [D, D]); di("bWk", [D, D]); di("bWv", [D, D]); di("bWo", [D, D])
        di("bWg1", [D, HD]); di("bWg2", [HD, D]); di("bnw", [128, NF])
        di("fgk", [128, NF, NEXP])
        di("fwi0", [NEXP, D, D]); di("fwi1", [NEXP, D, D]); di("fwo", [NEXP, D, D])
        self.d = dd
        self.out_d = nc.dram_tensor("out", [D, S], F32, kind="ExternalOutput").ap()
        self.x_scr = nc.dram_tensor("x_scr", [D, S], F32, kind="Internal").ap()
        self.onw_scr = nc.dram_tensor("onw_scr", [D, S], BF16, kind="Internal").ap()
        self.CCN = NH * FDIM * HD + NKV * 128 * 24
        self.cc_in = nc.dram_tensor("cc_in", [self.CCN], F32, kind="Internal").ap()
        self.cc_out = nc.dram_tensor("cc_out", [self.CCN], F32, kind="Internal").ap()
        self.dbg = {}
        for name in self.debug_out:
            self.dbg[name] = nc.dram_tensor(name, [128, NF, S], F32, kind="ExternalOutput").ap()

        self.c_ones_bf = nc.inline_tensor(np.ones((128, 128), dtype=bf16np), name="c_ones_bf")
        self.c_ident_bf = nc.inline_tensor(np.eye(128, dtype=bf16np), name="c_ident_bf")
        self.c_shift = nc.inline_tensor(
            np.ascontiguousarray(_shift_mats().transpose(2, 0, 1, 3).reshape(128, 6, 128)).astype(bf16np),
            name="c_shift")
        ii, jj = np.mgrid[0:128, 0:128]
        diffT = np.where(jj >= ii, (jj - ii).astype(np.float32), np.float32(1e30))
        self.c_diffT = nc.inline_tensor(diffT.astype(np.float32), name="c_diffT")
        rows = np.arange(128, dtype=np.float32)[:, None]
        crows = np.concatenate(
            [rows + 1.0, 127.0 - rows, np.full((128, 1), 128.0, np.float32),
             np.full((128, 1), -1e30, np.float32), np.full((128, 1), 1e-8, np.float32),
             np.full((128, 1), 1e-12, np.float32), np.full((128, 1), 1.0, np.float32),
             np.full((128, 1), LN_EPS, np.float32), np.full((128, 1), -1.0, np.float32)],
            axis=1).astype(np.float32)
        self.c_crows = nc.inline_tensor(crows, name="c_crows")
        self.c_onesf = nc.inline_tensor(np.ones((1, 128), dtype=np.float32), name="c_onesf")

        sb = lambda name, shape, dt: stk.enter_context(nc.sbuf_tensor(name, shape, dt))
        self.xn = sb("sb_xn", [128, NF, S], BF16)
        self.delta = sb("sb_delta", [128, NF, S], BF16)
        self.wring = sb("sb_wring", [128, NBUF, 2048], BF16)
        self.ones_bf = sb("sb_ones", [128, 128], BF16)
        self.ident_bf = sb("sb_ident", [128, 128], BF16)
        self.shift_bf = sb("sb_shift", [128, 6, 128], BF16)
        self.diffT_sb = sb("sb_diffT", [128, 128], F32)
        self.crows = sb("sb_crows", [128, 9], F32)
        self.onesf_row = sb("sb_onesf", [1, 128], F32)
        self.bq_sb = sb("sb_bq", [128, NF], F32)
        self.bk_sb = sb("sb_bk", [128, NKV], F32)
        self.bv_sb = sb("sb_bv", [128, NKV], F32)
        self.bg_sb = sb("sb_bg", [128, NF], F32)
        self.bmb_sb = sb("sb_bmb", [128, NF], F32)
        self.normT_sb = sb("sb_normT", [128, NFC, NH], F32)
        self.normT_bf = sb("sb_normTb", [128, NFC, NH], BF16)
        self.bnw_sb = sb("sb_bnw", [128, NF], F32)
        self.fgk_bf = sb("sb_fgk", [128, NF, NEXP], BF16)
        self.ld_row = sb("sb_ldrow", [1, NH], F32)
        self.ldb = sb("sb_ldb", [128, NH], F32)
        self.dumpb = sb("sb_dumpb", [128, 768], BF16)
        self.jdump = sb("sb_jdump", [128, 8], BF16)
        self.psBig = stk.enter_context(nc.psum_tensor("ps_big", [128, 4, 512], F32))
        self.psBig2 = stk.enter_context(nc.psum_tensor("ps_big2", [128, 4, 512], F32))

        mk = lambda name: Sem(nc, stk, name)
        self.dstr = [mk(f"s_d{i}") for i in range(16)]
        self.wc = mk("s_wc")
        self.mm = mk("s_mm")
        self.vv = mk("s_vv")
        self.se = mk("s_se")
        self.ge = mk("s_ge")
        self.cc = mk("s_cc")
        self.sst = self.dstr[4]
        self.joins = [mk(f"s_j{i}") for i in range(5)]
        self.join_n = 0
        self.const_tok = 0

        self.uid = 0
        self.wplan = []
        self.wemit = 0
        self.wtok = []
        self.aa_psT_free = 0
        self.waited = {w: {} for w in ("T", "V", "S", "G", "Y")}

        self.block = stk.enter_context(nc.Block())
        bl = self.block
        self._mk = {"T": bl.tensor, "V": bl.vector, "S": bl.scalar, "G": bl.gpsimd, "Y": bl.sync}

    def sdma(self, e, sidx, out, in_):
        s = self.dstr[sidx]
        tok = s.inc(e.dma_start(out=out, in_=in_), 16)
        return (sidx, tok)

    def sdma_accum(self, e, sidx, out, in_, accum):
        s = self.dstr[sidx]
        if accum:
            tok = s.inc(e.dma_start(out=out, in_=in_, accum_op=ALU.add), 16)
        else:
            tok = s.inc(e.dma_start(out=out, in_=in_), 16)
        return (sidx, tok)

    def swait(self, e, tok):
        if tok is None:
            return
        e.wait(self.dstr[tok[0]], tok[1])

    def sec(self, which, fn):
        holder = {}
        def run(e):
            holder["r"] = fn(Eng(e, which, self.waited[which]))
        self._mk[which](run)
        return holder.get("r")

    def wadd(self, src_ap):
        self.wplan.append(src_ap)
        return len(self.wplan) - 1

    def wloads_n(self, g: Eng, n):
        stop = min(self.wemit + n, len(self.wplan))
        while self.wemit < stop:
            i = self.wemit
            src = self.wplan[i]
            if i >= NBUF:
                g.wait(self.wc, i - NBUF + 1)
            nn = src.size() // 128
            dst = self.wring[:, i % NBUF, :nn]
            self.wtok.append(self.sdma(g, 6 + i % NBUF, dst, src))
            self.wemit += 1

    def wloads(self, g: Eng):
        while self.wemit < len(self.wplan):
            i = self.wemit
            src = self.wplan[i]
            if i >= NBUF:
                g.wait(self.wc, i - NBUF + 1)
            n = src.size() // 128
            dst = self.wring[:, i % NBUF, :n]
            self.wtok.append(self.sdma(g, 6 + i % NBUF, dst, src))
            self.wemit += 1

    def wget(self, t: Eng, i):
        self.swait(t, self.wtok[i])
        return self.wring[:, i % NBUF]

    def wdone(self, t: Eng):
        self.wc.inc(t.nop())

    def join(self):
        self.join_n += 1
        n = self.join_n
        for idx, which in enumerate(("T", "V", "S", "G", "Y")):
            j = self.joins[idx]
            def fin(e, j=j, which=which):
                if which == "T":
                    e.wait(self.wc, self.wc.n)
                    e.wait(self.mm, self.mm.n)
                    j.inc(e.nop())
                elif which == "Y":
                    for s in self.dstr:
                        e.wait(s, s.n)
                    j.inc(e.nop())
                elif which == "S":
                    j.inc(e.copy(self.jdump[0:1, 2:3], self.jdump[0:1, 4:5]))
                elif which == "V":
                    j.inc(e.memset(self.jdump[0:1, 0:1], 0.0))
                else:
                    j.inc(e.memset(self.jdump[0:1, 1:2], 0.0))
            self.sec(which, fin)
        for which in ("T", "V", "S", "G", "Y"):
            def wt(e):
                for j in self.joins:
                    e.wait(j, n)
            self.sec(which, wt)

    def emit_consts(self):
        d = self.d
        def y(e: Eng):
            dma = lambda dst, src: self.dstr[10].inc(e.dma_start(out=dst, in_=src), 16)
            dma(self.ones_bf[:], self.c_ones_bf.ap())
            dma(self.ident_bf[:], self.c_ident_bf.ap())
            dma(self.shift_bf[:], self.c_shift.ap())
            dma(self.diffT_sb[:], self.c_diffT.ap())
            dma(self.crows[:], self.c_crows.ap())
            dma(self.onesf_row[:], self.c_onesf.ap())
            dma(self.bq_sb[:], d["abq"])
            dma(self.bk_sb[:], d["abk"])
            dma(self.bv_sb[:], d["abv"])
            dma(self.bg_sb[:], d["abg"])
            dma(self.bmb_sb[:], d["abmb"])
            dma(self.normT_sb[:], d["anormT"])
            dma(self.bnw_sb[:], d["bnw"])
            dma(self.ld_row[0:1, :], d["bld"].unsqueeze(0))
            return self.dstr[10].n
        ldn = self.sec("Y", y)
        def g(e: Eng):
            self.dstr[11].inc(e.dma_start(out=self.normT_bf[:], in_=d["anormT"]), 16)
            self.dstr[11].inc(e.dma_start(out=self.fgk_bf[:], in_=d["fgk"]), 16)
            return self.dstr[11].n
        gldn = self.sec("G", g)
        self.sec("V", lambda e: e.memset(self.jdump[:], 0.0))
        self.const_tok = ldn
        self.gconst_tok = gldn

    def wconsts(self, e: Eng):
        e.wait(self.dstr[10], self.const_tok)
        e.wait(self.dstr[11], self.gconst_tok)

    def emit_boundary(self, src, dst, has_delta, write_xn):
        nc = self.nc
        self.uid += 1
        u = self.uid
        with contextlib.ExitStack() as ps:
            sb = lambda name, shape, dt: ps.enter_context(nc.sbuf_tensor(f"{name}_{u}", shape, dt))
            xl = sb("bn_xl", [128, 2, HALF], F32)
            xp = sb("bn_xp", [128, 2, HALF], F32)
            xcb = sb("bn_xcb", [128, 2, HALF], BF16)
            sqb = sb("bn_sqb", [128, 2, HALF], BF16)
            mB = sb("bn_mB", [128, HALF], F32)
            exq = sb("bn_exq", [128, HALF], F32)
            uu = sb("bn_uu", [128, HALF], F32)
            rB = sb("bn_rB", [128, HALF], F32)
            xl2 = sb("bn_xl2", [128, 2, HALF], F32)
            tmf = sb("bn_tmf", [128, 2, HALF], F32)
            psS = self.psBig.ap()[:, 0:2]
            psQ = self.psBig.ap()[:, 2:4]

            stats_done = 0
            prev_g15 = 0
            NT2 = 2 * NF
            x_tok = [0] * NT2
            st_tok = [None] * NT2
            se_tok = [0] * NT2
            mm_tok = [0] * NT2
            a_tok = [0] * NT2
            g_tok = [0] * NT2
            ld_tok = [None] * NT2
            ld2_tok = [None] * NT2
            for hf in range(2):
                th = slice(hf * HALF, (hf + 1) * HALF)
                for fc in range(NF):
                    gi = hf * NF + fc
                    par = fc % 2
                    rows = slice(fc * 128, (fc + 1) * 128)

                    def y1(e: Eng, fc=fc, gi=gi, par=par, rows=rows):
                        if gi >= 2:
                            if has_delta:
                                e.wait(self.vv, x_tok[gi - 2])
                            else:
                                if write_xn:
                                    e.wait(self.se, se_tok[gi - 2])
                                if dst is not None:
                                    self.swait(e, st_tok[gi - 2])
                        ld_tok[gi] = self.sdma(e, par, xl[:, par], src[rows, th])
                    self.sec("Y", y1)

                    if has_delta:
                        def v1(e: Eng, fc=fc, gi=gi, par=par):
                            self.swait(e, ld_tok[gi])
                            if gi >= 2:
                                if dst is not None:
                                    self.swait(e, st_tok[gi - 2])
                                if write_xn:
                                    e.wait(self.se, se_tok[gi - 2])
                            x_tok[gi] = self.vv.inc(
                                e.tensor_add(xp[:, par], xl[:, par], self.delta[:, fc, th]))
                        self.sec("V", v1)
                        xsrc = xp
                    else:
                        xsrc = xl

                    if dst is not None:
                        def y2(e: Eng, fc=fc, gi=gi, par=par, rows=rows, xsrc=xsrc):
                            if has_delta:
                                e.wait(self.vv, x_tok[gi])
                            else:
                                self.swait(e, ld_tok[gi])
                            st_tok[gi] = self.sdma(e, 4 + par, dst[rows, th], xsrc[:, par])
                        self.sec("Y", y2)

                    if write_xn:
                        def s1(e: Eng, fc=fc, gi=gi, par=par, xsrc=xsrc):
                            if has_delta:
                                e.wait(self.vv, x_tok[gi])
                            else:
                                self.swait(e, ld_tok[gi])
                            if gi >= 2:
                                e.wait(self.mm, mm_tok[gi - 2])
                            e.activation(out=xcb[:, par], in_=xsrc[:, par], func=AF.Copy)
                            se_tok[gi] = self.se.inc(
                                e.activation(out=sqb[:, par], in_=xsrc[:, par], func=AF.Square))
                        self.sec("S", s1)

                        def t1(e: Eng, fc=fc, gi=gi, par=par):
                            self.wconsts(e)
                            e.wait(self.se, se_tok[gi])
                            if fc == 0:
                                e.wait(self.vv, stats_done)
                            first, last = fc == 0, fc == NF - 1
                            for tt in range(2):
                                e.matmul(psS[:, tt], self.ones_bf[:], xcb[:, par, tt * 512:(tt + 1) * 512],
                                         start=first, stop=last)
                            inst = None
                            for tt in range(2):
                                inst = e.matmul(psQ[:, tt], self.ones_bf[:], sqb[:, par, tt * 512:(tt + 1) * 512],
                                                start=first, stop=last)
                            mm_tok[gi] = self.mm.inc(inst)
                        self.sec("T", t1)
                if not write_xn:
                    if dst is not None and hf == 1:
                        def ytail(e: Eng):
                            self.swait(e, st_tok[NT2 - 2])
                            self.swait(e, st_tok[NT2 - 1])
                        self.sec("Y", ytail)
                    continue

                def v2(e: Eng, hf=hf):
                    e.wait(self.mm, mm_tok[hf * NF + NF - 1])
                    e.wait(self.ge, prev_g15)
                    e.tensor_scalar(mB[:], psS.rearrange("p a b -> p (a b)"),
                                    scalar1=-1.0 / D, scalar2=None, op0=ALU.mult)
                    t1_ = self.vv.inc(e.tensor_scalar(exq[:], psQ.rearrange("p a b -> p (a b)"),
                                                      scalar1=1.0 / D, scalar2=None, op0=ALU.mult))
                    e.wait(self.vv, t1_)
                    t2_ = self.vv.inc(e.tensor_mul(uu[:], mB[:], mB[:]))
                    e.wait(self.vv, t2_)
                    return self.vv.inc(e.tensor_sub(exq[:], exq[:], uu[:]))
                var_tok = self.sec("V", v2)
                stats_done = var_tok

                def s2(e: Eng):
                    e.wait(self.vv, var_tok)
                    return self.se.inc(e.activation(out=uu[:], in_=exq[:], func=AF.Sqrt,
                                                    bias=self.crows[:, C_LNEPS:C_LNEPS + 1]))
                srt_tok = self.sec("S", s2)

                def v3(e: Eng):
                    e.wait(self.se, srt_tok)
                    return self.vv.inc(e.reciprocal(rB[:], uu[:]))
                r_tok = self.sec("V", v3)

                src2 = dst if dst is not None else src
                for fc in range(NF):
                    gi = hf * NF + fc
                    par = fc % 2
                    rows = slice(fc * 128, (fc + 1) * 128)

                    def y3(e: Eng, fc=fc, gi=gi, par=par, rows=rows):
                        if dst is not None:
                            self.swait(e, st_tok[gi])
                        if gi >= 2:
                            e.wait(self.vv, a_tok[gi - 2])
                        ld2_tok[gi] = self.sdma(e, 2 + par, xl2[:, par], src2[rows, th])
                    self.sec("Y", y3)

                    def v4(e: Eng, fc=fc, gi=gi, par=par, r_tok=r_tok):
                        self.swait(e, ld2_tok[gi])
                        e.wait(self.vv, r_tok)
                        if gi >= 2:
                            e.wait(self.ge, g_tok[gi - 2])
                        a_tok[gi] = self.vv.inc(e.tensor_add(tmf[:, par], xl2[:, par], mB[:]))
                    self.sec("V", v4)

                    def g1(e: Eng, fc=fc, gi=gi, par=par, th=th):
                        e.wait(self.vv, a_tok[gi])
                        g_tok[gi] = self.ge.inc(e.tensor_mul(self.xn[:, fc, th], tmf[:, par], rB[:]))
                    self.sec("G", g1)
                prev_g15 = g_tok[hf * NF + NF - 1]
        self.join()

    def emit_dump(self, name, which):
        if name not in self.dbg:
            return
        src = {"xn": self.xn, "delta": self.delta}[which]
        nc = self.nc
        with contextlib.ExitStack() as ps:
            self.uid += 1
            tmp = ps.enter_context(nc.sbuf_tensor(f"dbg_tmp_{self.uid}", [128, S], F32))
            for fc in range(NF):
                def vc(e: Eng, fc=fc):
                    return self.vv.inc(e.tensor_copy(tmp[:], src[:, fc]))
                t = self.sec("V", vc)
                def yd(e: Eng, fc=fc, t=t):
                    e.wait(self.vv, t)
                    return self.sdma(e, 4, self.dbg[name][:, fc], tmp[:])
                t2 = self.sec("Y", yd)
                def vw(e: Eng, t2=t2):
                    self.swait(e, t2)
                self.sec("V", vw)
        self.join()

    def emit_ffn(self):
        nc = self.nc
        d = self.d
        self.uid += 1
        u = self.uid
        with contextlib.ExitStack() as ps:
            sb = lambda name, shape, dt: ps.enter_context(nc.sbuf_tensor(f"{name}_{u}", shape, dt))
            hh = sb("ff_h", [128, NF, HALF], BF16)
            htmp = sb("ff_htmp", [128, 2, HALF], BF16)
            cwb = sb("ff_cwb", [128, S], BF16)
            glog = sb("ff_glog", [128, NF, NEXP], F32)
            m1 = sb("ff_m1", [128, NF], F32)
            m2 = sb("ff_m2", [128, NF], F32)
            eq1 = sb("ff_eq1", [128, NF, NEXP], F32)
            eq2 = sb("ff_eq2", [128, NF, NEXP], F32)
            gtmp = sb("ff_gtmp", [128, NF, NEXP], F32)
            dfe = sb("ff_dfe", [128, NF], F32)
            w1 = sb("ff_w1", [128, NF], F32)
            w2 = sb("ff_w2", [128, NF], F32)
            cwf = sb("ff_cwf", [128, NF, NEXP], F32)
            cw_bf = sb("ff_cwbf", [128, NF, NEXP], BF16)

            NEXP_LIM = int(os.environ.get("KF_NEXP_LIM", NEXP))
            SKIP_GATING = os.environ.get("KF_SKIP_GATING", "") == "1"
            SKIP_CWB = os.environ.get("KF_SKIP_CWB", "") == "1"
            if SKIP_GATING:
                def vq(e: Eng):
                    t = self.vv.inc(e.memset(cwf[:], 0.25))
                    e.wait(self.vv, t)
                    return self.vv.inc(e.tensor_copy(cw_bf[:], cwf[:]))
                cw_tok = self.sec("V", vq)
            psA = self.psBig.ap()[:, 0:2]
            psB = self.psBig.ap()[:, 2:4]
            pscw = self.psBig2.ap().rearrange("p a b -> p (a b)")
            psg = pscw[:, 0:128].rearrange("p (a b) -> p a b", b=NEXP)
            if True:
                if SKIP_GATING:
                    pass
                else:
                    def t0(e: Eng):
                        self.wconsts(e)
                        inst = None
                        for tc in range(NF):
                            for fc in range(NF):
                                inst = e.matmul(psg[:, tc, :], self.xn[:, fc, tc * 128:(tc + 1) * 128],
                                                self.fgk_bf[:, fc], start=(fc == 0), stop=(fc == NF - 1))
                        return self.mm.inc(inst)
                    gmm = self.sec("T", t0)

                    def vg(e: Eng):
                        e.wait(self.mm, gmm)
                        self.wconsts(e)
                        t = self.vv.inc(e.tensor_copy(glog[:], psg))
                        e.wait(self.vv, t)
                        t = self.vv.inc(e.reduce_max(m1[:], glog[:], axis=AX.X))
                        e.wait(self.vv, t)
                        t = self.vv.inc(e.tensor_tensor(
                            eq1[:], glog[:], m1[:].unsqueeze(2).broadcast_to([128, NF, NEXP]), ALU.is_equal))
                        e.wait(self.vv, t)
                        t = self.vv.inc(e.scalar_tensor_tensor(
                            out=gtmp[:], in0=eq1[:], scalar=self.crows[:, C_N1E30:C_N1E30 + 1], in1=glog[:],
                            op0=ALU.mult, op1=ALU.add))
                        e.wait(self.vv, t)
                        t = self.vv.inc(e.reduce_max(m2[:], gtmp[:], axis=AX.X))
                        e.wait(self.vv, t)
                        e.tensor_tensor(eq2[:], gtmp[:], m2[:].unsqueeze(2).broadcast_to([128, NF, NEXP]),
                                        ALU.is_equal)
                        t = self.vv.inc(e.tensor_sub(dfe[:], m2[:], m1[:]))
                        e.wait(self.vv, t)
                        return t
                    dfe_tok = self.sec("V", vg)

                    def sg(e: Eng):
                        e.wait(self.vv, dfe_tok)
                        return self.se.inc(e.activation(out=dfe[:], in_=dfe[:], func=AF.Exp))
                    e2_tok = self.sec("S", sg)

                    def vg2(e: Eng):
                        e.wait(self.se, e2_tok)
                        t = self.vv.inc(e.tensor_scalar(w1[:], dfe[:], scalar1=1.0, scalar2=None, op0=ALU.add))
                        e.wait(self.vv, t)
                        t = self.vv.inc(e.reciprocal(w1[:], w1[:]))
                        e.wait(self.vv, t)
                        t = self.vv.inc(e.tensor_mul(w2[:], dfe[:], w1[:]))
                        e.wait(self.vv, t)
                        e.tensor_tensor(gtmp[:], eq2[:], w2[:].unsqueeze(2).broadcast_to([128, NF, NEXP]), ALU.mult)
                        t = self.vv.inc(e.tensor_tensor(
                            cwf[:], eq1[:], w1[:].unsqueeze(2).broadcast_to([128, NF, NEXP]), ALU.mult))
                        e.wait(self.vv, t)
                        t = self.vv.inc(e.tensor_add(cwf[:], cwf[:], gtmp[:]))
                        e.wait(self.vv, t)
                        return self.vv.inc(e.tensor_copy(cw_bf[:], cwf[:]))
                    cw_tok = self.sec("V", vg2)

            PH = os.environ.get("KF_PHASES", "h0,h1,gh,wo").split(",")
            mats = [m for m in range(3) if ("h0", "h1", "wo")[m] in PH]
            blocks = {}
            for e_ in range(NEXP_LIM):
                for hf in range(2):
                    for mat, wname in ((0, "fwi0"), (1, "fwi1"), (2, "fwo")):
                        if mat not in mats:
                            continue
                        for mc in range(NF):
                            blocks[(e_, hf, mat, mc)] = self.wadd(
                                d[wname][e_, :, mc * 128:(mc + 1) * 128]
                                .rearrange("(fc p) m -> p fc m", p=128))
            self.sec("G", lambda g: self.wloads(g))

            nblk = [self.wc.n]
            blk0 = self.wc.n
            cwb_mm = [0]
            cwb_tok = [0]
            h1ev = {}
            hev = {}
            silu = {}
            blk_ev = {}

            for e_ in range(NEXP_LIM):
                def tcw(e: Eng, e_=e_):
                    if SKIP_CWB:
                        return
                    e.wait(self.vv, cw_tok)
                    if e_ > 0:
                        e.wait(self.vv, hev[(1, NF - 1)])
                    inst = None
                    for tc in range(NF):
                        inst = e.matmul(pscw[:, tc * 128:(tc + 1) * 128],
                                        cw_bf[:, tc, e_:e_ + 1].broadcast_to([128, 128]),
                                        self.ident_bf[:], start=True, stop=True)
                    cwb_mm[0] = self.mm.inc(inst)
                self.sec("T", tcw)

                def vcw(e: Eng):
                    if SKIP_CWB:
                        cwb_tok[0] = self.vv.inc(e.memset(cwb[:], 0.25))
                        return
                    e.wait(self.mm, cwb_mm[0])
                    cwb_tok[0] = self.vv.inc(e.tensor_copy(cwb[:], pscw[:]))
                self.sec("V", vcw)

                for hf in range(2):
                    th = slice(hf * HALF, (hf + 1) * HALF)

                    def th0(e: Eng, e_=e_, hf=hf):
                        se_base = self.se.n
                        for mc in range(NF):
                            i = nblk[0]
                            w = self.wget(e, blocks[(e_, hf, 0, mc)]).rearrange("p (fc m) -> p fc m", m=128)
                            if i - blk0 >= 2:
                                s_, t_ = blk_ev[i - 2]
                                e.wait(s_, t_)
                            ps_ = (psA, psB)[i % 2]
                            inst = None
                            for fc in range(NF):
                                for tt in range(2):
                                    inst = e.matmul(
                                        ps_[:, tt], w[:, fc],
                                        self.xn[:, fc, hf * HALF + tt * 512: hf * HALF + (tt + 1) * 512],
                                        start=(fc == 0), stop=(fc == NF - 1))
                            self.wc.inc(inst)
                            blk_ev[i] = (self.se, se_base + mc + 1)
                            nblk[0] += 1

                    def sh0(e: Eng, e_=e_, hf=hf):
                        base = nblk[0] - NF
                        for mc in range(NF):
                            i = base + mc
                            e.wait(self.wc, i + 1)
                            if (hf, mc) in hev:
                                e.wait(self.vv, hev[(hf, mc)])
                            ps_ = (psA, psB)[i % 2]
                            silu[(hf, mc)] = self.se.inc(
                                e.activation(out=hh[:, mc], in_=ps_[:].rearrange("p a b -> p (a b)"),
                                             func=AF.Silu))
                            assert silu[(hf, mc)] == blk_ev[i][1], "se token prediction drift"

                    if "h0" in PH:
                        self.sec("T", th0)
                        self.sec("S", sh0)

                    def th1_(e: Eng, e_=e_, hf=hf):
                        vv_base = self.vv.n
                        for mc in range(NF):
                            i = nblk[0]
                            w = self.wget(e, blocks[(e_, hf, 1, mc)]).rearrange("p (fc m) -> p fc m", m=128)
                            if i - blk0 >= 2:
                                s_, t_ = blk_ev[i - 2]
                                e.wait(s_, t_)
                            ps_ = (psA, psB)[i % 2]
                            inst = None
                            for fc in range(NF):
                                for tt in range(2):
                                    inst = e.matmul(
                                        ps_[:, tt], w[:, fc],
                                        self.xn[:, fc, hf * HALF + tt * 512: hf * HALF + (tt + 1) * 512],
                                        start=(fc == 0), stop=(fc == NF - 1))
                            self.wc.inc(inst)
                            blk_ev[i] = (self.vv, vv_base + 2 * mc + 1)
                            nblk[0] += 1

                    def vh1(e: Eng, e_=e_, hf=hf):
                        base = nblk[0] - NF
                        for mc in range(NF):
                            i = base + mc
                            e.wait(self.wc, i + 1)
                            e.wait(self.vv, cwb_tok[0])
                            e.wait(self.se, silu[(hf, mc)])
                            t_ = self.vv.inc(
                                e.tensor_tensor(htmp[:, mc % 2], ps_flatten((psA, psB)[i % 2]),
                                                cwb[:, th], ALU.mult))
                            assert t_ == blk_ev[i][1], "vv token prediction drift"
                            e.wait(self.vv, t_)
                            hev[(hf, mc)] = self.vv.inc(
                                e.tensor_mul(hh[:, mc], hh[:, mc], htmp[:, mc % 2]))
                            h1ev[(hf, mc)] = hev[(hf, mc)]

                    if "h1" in PH:
                        self.sec("T", th1_)
                        self.sec("V", vh1)

                    def two(e: Eng, e_=e_, hf=hf):
                        vv_base = self.vv.n
                        for dc in range(NF):
                            i = nblk[0]
                            w = self.wget(e, blocks[(e_, hf, 2, dc)]).rearrange("p (mc dd) -> p mc dd", dd=128)
                            if i - blk0 >= 2:
                                s_, t_ = blk_ev[i - 2]
                                e.wait(s_, t_)
                            if dc == 0:
                                e.wait(self.vv, hev[(hf, NF - 1)])
                            ps_ = (psA, psB)[i % 2]
                            inst = None
                            for mc in range(NF):
                                for tt in range(2):
                                    inst = e.matmul(ps_[:, tt], w[:, mc],
                                                    hh[:, mc, tt * 512:(tt + 1) * 512],
                                                    start=(mc == 0), stop=(mc == NF - 1))
                            self.wc.inc(inst)
                            blk_ev[i] = (self.vv, vv_base + dc + 1)
                            nblk[0] += 1

                    def vwo(e: Eng, e_=e_, hf=hf):
                        base = nblk[0] - NF
                        for dc in range(NF):
                            i = base + dc
                            e.wait(self.wc, i + 1)
                            ps_ = (psA, psB)[i % 2]
                            pflat = ps_[:].rearrange("p a b -> p (a b)")
                            if e_ == 0:
                                t = self.vv.inc(e.tensor_copy(self.delta[:, dc, th], pflat))
                            else:
                                t = self.vv.inc(e.tensor_add(self.delta[:, dc, th],
                                                             self.delta[:, dc, th], pflat))
                            assert t == blk_ev[i][1], "vv token prediction drift (wo)"

                    if "wo" in PH:
                        self.sec("T", two)
                        self.sec("V", vwo)
        self.join()

    def emit_attn_a(self):
        nc = self.nc
        d = self.d
        self.uid += 1
        u = self.uid
        SH = 8

        kblk = {}
        for kv in range(NKV):
            for sh in range(2):
                kblk[(kv, sh, "k")] = self.wadd(
                    d["aWkT"][:, kv * 128:(kv + 1) * 128].rearrange("(fc p) m -> p fc m", p=128))
                kblk[(kv, sh, "v")] = self.wadd(
                    d["aWvT"][:, kv * 128:(kv + 1) * 128].rearrange("(fc p) m -> p fc m", p=128))
                for h4 in range(4):
                    h = kv * 4 + h4
                    kblk[(kv, sh, "mb", h4)] = self.wadd(
                        d["aWmbT"][:, h * 128:(h + 1) * 128].rearrange("(fc p) m -> p fc m", p=128))
        qblk = {}
        for h in range(NH):
            qblk[(h, "q")] = self.wadd(
                d["aWqT"][:, h * 128:(h + 1) * 128].rearrange("(fc p) m -> p fc m", p=128))
            qblk[(h, "g")] = self.wadd(
                d["aWgT"][:, h * 128:(h + 1) * 128].rearrange("(fc p) m -> p fc m", p=128))

        with contextlib.ExitStack() as ps:
            sb = lambda name, shape, dt: ps.enter_context(nc.sbuf_tensor(f"{name}_{u}", shape, dt))
            kfm = sb("aa_fm", [128, HALF], BF16)
            vfm = kfm
            mbfm = kfm
            ktm = sb("aa_ktm", [128, SH, 128], BF16)
            vtm = sb("aa_vtm", [128, SH, 128], BF16)
            mbtm = sb("aa_mbtm", [128, SH, 128], BF16)
            base = sb("aa_base", [128, SH, 256], BF16)
            mk_tm = sb("aa_mktm", [128, SH, FDIM], BF16)
            mk_fm = sb("aa_mkfm", [128, NFC, HALF], BF16)
            mem_sb = sb("aa_mem", [128, NFC, 512], BF16)
            num_bf = sb("aa_num", [128, SH, 512], BF16)
            Sall = sb("aa_Sall", [128, SH], F32)
            uua = sb("aa_u", [128, SH], F32)
            rra = sb("aa_r", [128, SH], F32)
            sinv = sb("aa_sinv", [128, SH], F32)
            u1e8 = sb("aa_u1e8", [128, SH], F32)
            usv = sb("aa_usv", [128, SH], F32)
            denf = sb("aa_den", [128, SH, 4], F32)
            denr = sb("aa_denr", [128, SH, 4], F32)
            coefb = sb("aa_coef", [128, SH, 4], BF16)
            wmv = sb("aa_wmv", [128, SH, 128], BF16)
            pvt = sb("aa_pv", [128, SH, 128], BF16)
            dmem_st = sb("aa_dst", [128, NFC, 128], F32)
            dn_acc = sb("aa_dna", [128, NFC, 4], F32)
            psP = self.psBig.ap()[:, 0:2]
            psN = self.psBig.ap()[:, 2]
            psDen = self.psBig.ap()[:, 3, 0:32]
            psT = self.psBig2.ap()[:, 3].bitcast(BF16)[:, 0:512]
            psDM = self.psBig2.ap().rearrange("p a b -> p (a b)")[:, 0:768].rearrange(
                "p (a b) -> p a b", b=128)
            psDN = self.psBig2.ap()[:, 2, 0:24].rearrange("p (a b) -> p a b", b=4)

            self.aa_psP_free = 0
            self.aa_psT_free = 0
            self.aa_psN_last = 0
            self.aa_psDen_free = 0
            self.aa_dm_ev = 0
            self.aa_dn_ev = 0
            self.aa_sq_last = 0
            self.aa_wmv_free = 0
            self.aa_mk_readers = 0
            self.aa_mkfm_readers = 0
            self.aa_fm_free = 0

            mem_tok = [None] * NKV
            kv_done_v = [0] * NKV
            cc_store_toks = []
            dmem_store = {}
            dn_store = {}
            dmev = {}

            def proj(which, kv, sh, blk, fmbuf, bias_ap, act):
                def tp(e: Eng):
                    w = self.wget(e, blk).rearrange("p (fc m) -> p fc m", m=128)
                    e.wait(self.se, self.aa_psP_free)
                    inst = None
                    for fc in range(NF):
                        for tt in range(2):
                            inst = e.matmul(psP[:, tt], w[:, fc],
                                            self.xn[:, fc, sh * HALF + tt * 512: sh * HALF + (tt + 1) * 512],
                                            start=(fc == 0), stop=(fc == NF - 1))
                    self.wc.inc(inst)
                    return self.wc.n
                ptok = self.sec("T", tp)

                def sev(e: Eng):
                    e.wait(self.wc, ptok)
                    e.wait(self.vv, self.aa_fm_free)
                    t = self.se.inc(e.activation(out=fmbuf[:], in_=ps_flatten(psP), func=act,
                                                 bias=bias_ap))
                    self.aa_psP_free = t
                    return t
                return self.sec("S", sev)

            for kv in range(NKV):
                def gmem(e: Eng, kv=kv):
                    if kv >= 1:
                        e.wait(self.vv, kv_done_v[kv - 1])
                    toks = []
                    for h4 in range(4):
                        sidx = 12 if h4 % 2 == 0 else 15
                        if h4 >= 2:
                            self.swait(e, toks[h4 - 2])
                        toks.append(self.sdma(
                            e, sidx, mem_sb[:, :, h4 * 128:(h4 + 1) * 128],
                            d["amem"][kv * 4 + h4].rearrange("(fcc p) dd -> p fcc dd", p=128)))
                    mem_tok[kv] = toks
                self.sec("G", gmem)

                for sh in range(2):
                    th = slice(sh * HALF, (sh + 1) * HALF)
                    self.sec("G", lambda g: self.wloads_n(g, 6))

                    ket = proj("k", kv, sh, kblk[(kv, sh, "k")], kfm, self.bk_sb[:, kv:kv + 1], AF.Identity)
                    prs = [(kfm[:, tc * 128:(tc + 1) * 128], ktm[:, tc]) for tc in range(SH)]
                    ktm_tok = self.emit_transposes_after_se(prs, psT, ket,
                                                           dst_guard=("se", self.aa_sq_last))
                    self.aa_fm_free = ktm_tok

                    vet = proj("v", kv, sh, kblk[(kv, sh, "v")], vfm, self.bv_sb[:, kv:kv + 1], AF.Identity)
                    prs = [(vfm[:, tc * 128:(tc + 1) * 128], vtm[:, tc]) for tc in range(SH)]
                    vtm_tok = self.emit_transposes_after_se(prs, psT, vet)
                    self.aa_fm_free = vtm_tok

                    def sbase(e: Eng, ktm_tok=ktm_tok):
                        e.wait(self.vv, ktm_tok)
                        e.wait(self.se, self.aa_sq_last)
                        e.wait(self.mm, self.aa_mk_readers)
                        e.activation(out=base[:, :, 0:128], in_=ktm[:], func=AF.Relu)
                        return self.se.inc(e.activation(out=base[:, :, 128:256], in_=ktm[:],
                                                        func=AF.Relu,
                                                        scale=self.crows[:, C_NEG1:C_NEG1 + 1]))
                    btok = self.sec("S", sbase)

                    def vmk(e: Eng, btok=btok):
                        e.wait(self.se, btok)
                        t = 0
                        for j in (1, 2, 3):
                            o = (j - 1) * 256
                            t = self.vv.inc(e.tensor_mul(
                                mk_tm[:, :, o + j:o + 256], base[:, :, j:256], base[:, :, 0:256 - j]))
                            t = self.vv.inc(e.tensor_mul(
                                mk_tm[:, :, o:o + j], base[:, :, 0:j], base[:, :, 256 - j:256]))
                        return t
                    mk_all = self.sec("V", vmk)
                    mk_toks = [mk_all] * SH

                    sq_toks = [0] * SH
                    for tc in range(SH):
                        def ssq(e: Eng, tc=tc, mk_all=mk_all):
                            e.wait(self.vv, mk_all)
                            if tc >= 1:
                                e.wait(self.se, sq_toks[tc - 1])
                            return self.se.inc(e.activation(out=self.dumpb[:], in_=mk_tm[:, tc],
                                                            func=AF.Square,
                                                            accum_out=Sall[:, tc:tc + 1]))
                        sq_toks[tc] = self.sec("S", ssq)
                    self.aa_sq_last = sq_toks[SH - 1]

                    def sch(e: Eng):
                        e.wait(self.se, sq_toks[SH - 1])
                        return self.se.inc(e.activation(out=uua[:], in_=Sall[:], func=AF.Sqrt))
                    ntok = self.sec("S", sch)

                    def vch(e: Eng, ntok=ntok):
                        e.wait(self.se, ntok)
                        t = self.vv.inc(e.tensor_scalar(uua[:], uua[:],
                                                        scalar1=self.crows[:, C_1E12:C_1E12 + 1],
                                                        scalar2=None, op0=ALU.max))
                        e.wait(self.vv, t)
                        t = self.vv.inc(e.reciprocal(rra[:], uua[:]))
                        t = self.vv.inc(e.reciprocal(sinv[:], Sall[:]))
                        e.wait(self.vv, t)
                        t = self.vv.inc(e.tensor_scalar(u1e8[:], uua[:],
                                                        scalar1=self.crows[:, C_1E8:C_1E8 + 1],
                                                        scalar2=None, op0=ALU.mult))
                        t = self.vv.inc(e.tensor_mul(usv[:], uua[:], sinv[:]))
                        return t
                    sch_tok = self.sec("V", vch)

                    prs = []
                    for fcc in range(NFC):
                        for tc in range(SH):
                            prs.append((mk_tm[:, tc, fcc * 128:(fcc + 1) * 128],
                                        mk_fm[:, fcc, tc * 128:(tc + 1) * 128]))
                    mkfm_tok = self.emit_transposes_after_vv(prs, psT, mk_toks[SH - 1],
                                                             dst_guard=("mm", self.aa_mkfm_readers))

                    num_pred = self.vv.n + 1
                    def tnd(e: Eng, kv=kv, mkfm_tok=mkfm_tok, num_pred=num_pred):
                        for tk in mem_tok[kv]:
                            self.swait(e, tk)
                        e.wait(self.vv, mkfm_tok)
                        e.wait(self.vv, self.aa_psDen_free)
                        for tc in range(SH):
                            if tc == 0:
                                e.wait(self.vv, self.aa_psN_last)
                            else:
                                e.wait(self.vv, num_pred + tc - 1)
                            inst = None
                            for h4 in range(4):
                                for fcc in range(NFC):
                                    e.matmul(psN[:, h4 * 128:(h4 + 1) * 128],
                                             mk_fm[:, fcc, tc * 128:(tc + 1) * 128],
                                             mem_sb[:, fcc, h4 * 128:(h4 + 1) * 128],
                                             start=(fcc == 0), stop=(fcc == NFC - 1))
                            for fcc in range(NFC):
                                inst = e.matmul(psDen[:, tc * 4:(tc + 1) * 4],
                                                mk_fm[:, fcc, tc * 128:(tc + 1) * 128],
                                                self.normT_bf[:, fcc, kv * 4:(kv + 1) * 4],
                                                start=(fcc == 0), stop=(fcc == NFC - 1))
                            self.mm.inc(inst)
                        return self.mm.n
                    nd_tok = self.sec("T", tnd)

                    def vnum(e: Eng, nd_tok=nd_tok, num_pred=num_pred):
                        base_mm = nd_tok - SH
                        t = 0
                        for tc in range(SH):
                            e.wait(self.mm, base_mm + tc + 1)
                            t = self.vv.inc(e.tensor_copy(num_bf[:, tc], psN))
                            assert t == num_pred + tc, "num evac token drift"
                        self.aa_psN_last = t
                        return t
                    self.sec("V", vnum)

                    def vden(e: Eng, nd_tok=nd_tok, sch_tok=sch_tok):
                        e.wait(self.mm, nd_tok)
                        e.wait(self.vv, sch_tok)
                        t = self.vv.inc(e.tensor_tensor(
                            denf[:], psDen.rearrange("p (a b) -> p a b", b=4),
                            u1e8[:].unsqueeze(2).broadcast_to([128, SH, 4]), ALU.add))
                        self.aa_psDen_free = t
                        e.wait(self.vv, t)
                        t = self.vv.inc(e.reciprocal(denr[:], denf[:]))
                        e.wait(self.vv, t)
                        t = self.vv.inc(e.tensor_tensor(
                            denf[:], denf[:], usv[:].unsqueeze(2).broadcast_to([128, SH, 4]), ALU.mult))
                        e.wait(self.vv, t)
                        t = self.vv.inc(e.tensor_scalar(denf[:], denf[:], scalar1=-1.0, scalar2=1.0,
                                                        op0=ALU.mult, op1=ALU.add))
                        e.wait(self.vv, t)
                        t = self.vv.inc(e.tensor_scalar(denf[:], denf[:], scalar1=0.0, scalar2=1.0,
                                                        op0=ALU.max, op1=ALU.min))
                        e.wait(self.vv, t)
                        t = self.vv.inc(e.tensor_tensor(
                            coefb[:], denf[:], rra[:].unsqueeze(2).broadcast_to([128, SH, 4]), ALU.mult))
                        return t
                    coef_tok = self.sec("V", vden)

                    for h4 in range(4):
                        h = kv * 4 + h4
                        met = proj("mb", kv, sh, kblk[(kv, sh, "mb", h4)], mbfm,
                                   self.bmb_sb[:, h:h + 1], AF.Sigmoid)
                        prs = [(mbfm[:, tc * 128:(tc + 1) * 128], mbtm[:, tc]) for tc in range(SH)]
                        mbtm_tok = self.emit_transposes_after_se(prs, psT, met)
                        self.aa_fm_free = mbtm_tok

                        def vwmv(e: Eng, h4=h4, mbtm_tok=mbtm_tok, coef_tok=coef_tok,
                                 vtm_tok=vtm_tok):
                            e.wait(self.vv, mbtm_tok)
                            e.wait(self.vv, coef_tok)
                            e.wait(self.vv, vtm_tok)
                            e.wait(self.mm, self.aa_wmv_free)
                            t = self.vv.inc(e.tensor_tensor(
                                pvt[:], num_bf[:, :, h4 * 128:(h4 + 1) * 128],
                                denr[:, :, h4:h4 + 1].broadcast_to([128, SH, 128]), ALU.mult))
                            e.wait(self.vv, t)
                            t = self.vv.inc(e.tensor_sub(pvt[:], vtm[:], pvt[:]))
                            e.wait(self.vv, t)
                            t = self.vv.inc(e.tensor_mul(pvt[:], pvt[:], mbtm[:]))
                            e.wait(self.vv, t)
                            t = self.vv.inc(e.tensor_tensor(
                                wmv[:], pvt[:],
                                rra[:].unsqueeze(2).broadcast_to([128, SH, 128]), ALU.mult))
                            e.wait(self.vv, t)
                            return t
                        wmv_tok = self.sec("V", vwmv)

                        def tdm(e: Eng, wmv_tok=wmv_tok):
                            e.wait(self.vv, wmv_tok)
                            e.wait(self.vv, self.aa_dm_ev)
                            inst = None
                            for fcc in range(NFC):
                                for tc in range(SH):
                                    inst = e.matmul(psDM[:, fcc],
                                                    mk_tm[:, tc, fcc * 128:(fcc + 1) * 128],
                                                    wmv[:, tc], start=(tc == 0), stop=(tc == SH - 1))
                            t = self.mm.inc(inst)
                            self.aa_wmv_free = t
                            return t
                        dm_tok = self.sec("T", tdm)

                        def vdm(e: Eng, kv=kv, sh=sh, h4=h4, dm_tok=dm_tok):
                            e.wait(self.mm, dm_tok)
                            if cc_store_toks:
                                self.swait(e, cc_store_toks[-1])
                            t = self.vv.inc(e.tensor_copy(dmem_st[:], psDM))
                            self.aa_dm_ev = t
                            return t
                        dmev[(kv, h4)] = self.sec("V", vdm)

                        def gdm(e: Eng, kv=kv, sh=sh, h4=h4):
                            h = kv * 4 + h4
                            e.wait(self.vv, dmev[(kv, h4)])
                            if cc_store_toks:
                                self.swait(e, cc_store_toks[-1])
                            tok = self.sdma_accum(
                                e, 13, self.cc_in[h * FDIM * HD:(h + 1) * FDIM * HD]
                                .rearrange("(fcc p dd) -> p fcc dd", p=128, dd=128),
                                dmem_st[:], accum=(sh == 1))
                            cc_store_toks.append(tok)
                        self.sec("G", gdm)

                    def tdn(e: Eng, coef_tok=coef_tok):
                        e.wait(self.vv, coef_tok)
                        e.wait(self.vv, self.aa_dn_ev)
                        inst = None
                        for fcc in range(NFC):
                            for tc in range(SH):
                                inst = e.matmul(psDN[:, fcc], mk_tm[:, tc, fcc * 128:(fcc + 1) * 128],
                                                coefb[:, tc], start=(tc == 0), stop=(tc == SH - 1))
                        return self.mm.inc(inst)
                    dn_tok = self.sec("T", tdn)
                    self.aa_mk_readers = dn_tok
                    self.aa_mkfm_readers = nd_tok

                    def vdn(e: Eng, kv=kv, sh=sh, dn_tok=dn_tok):
                        e.wait(self.mm, dn_tok)
                        if sh == 0:
                            if kv >= 1:
                                self.swait(e, dn_store.get(kv - 1))
                            t = self.vv.inc(e.tensor_copy(dn_acc[:], psDN))
                        else:
                            t = self.vv.inc(e.tensor_add(dn_acc[:], dn_acc[:], psDN))
                        self.aa_dn_ev = t
                        return t
                    dnev = self.sec("V", vdn)
                kv_done_v[kv] = dnev

                def gcc2(e: Eng, kv=kv, dnev=dnev):
                    e.wait(self.vv, dnev)
                    if cc_store_toks:
                        self.swait(e, cc_store_toks[-1])
                    tok = self.sdma(
                        e, 13, self.cc_in[NH * FDIM * HD + kv * 3072:NH * FDIM * HD + (kv + 1) * 3072]
                        .rearrange("(p j) -> p j", j=24),
                        dn_acc[:].rearrange("p a b -> p (a b)"))
                    cc_store_toks.append(tok)
                    dn_store[kv] = tok
                self.sec("G", gcc2)

            def gcc(e: Eng):
                for tok in cc_store_toks:
                    self.swait(e, tok)
                flat_rows = self.CCN // 128
                per = 1024
                nch = (flat_rows + per - 1) // per
                fake = os.environ.get("KF_CC_FAKE", "") == "1"
                for i in range(nch):
                    r0, r1 = i * per, min((i + 1) * per, flat_rows)
                    src_ap = self.cc_in[r0 * 128:r1 * 128].rearrange("(r c) -> r c", c=128).opt()
                    dst_ap = self.cc_out[r0 * 128:r1 * 128].rearrange("(r c) -> r c", c=128).opt()
                    if fake:
                        self.cc.inc(e.dma_start(out=dst_ap, in_=src_ap), 16)
                    else:
                        e.collective_compute(
                            "AllReduce", ALU.add, replica_groups=[list(range(B))],
                            ins=[src_ap], outs=[dst_ap],
                        ).then_inc(self.cc.h)
                        self.cc.n += 1
                return self.cc.n
            cc_all = self.sec("G", gcc)
        self.join()

        with contextlib.ExitStack() as ps:
            sb = lambda name, shape, dt: ps.enter_context(nc.sbuf_tensor(f"{name}_{u}", shape, dt))
            qfm = sb("aq_qfm", [128, S], BF16)
            qbase = sb("aq_base", [128, 2, HALF], BF16)
            mq = sb("aq_mq", [128, NFC, HALF], BF16)
            sqq = sb("aq_sq", [128, HALF], BF16)
            nbu = sb("aq_nb", [128, HALF], F32)
            dau = sb("aq_da", [128, HALF], F32)
            memh = sb("aq_memh", [128, HALF], BF16)
            gbf = sb("aq_gbf", [128, HALF], BF16)
            t3b = sb("aq_t3", [128, HALF], BF16)
            nn_rb = sb("aq_nnrb", [128, NKV, NFC, 4], F32)
            nn_f32 = sb("aq_nnf", [128, NFC, NH], F32)
            nn_bf = sb("aq_nnb", [128, NH, NFC], BF16)
            nmta = sb("aq_nmta", [128, 2, NFC, 128], F32)
            nmtb = sb("aq_nmtb", [128, 2, NFC, 128], F32)
            nm_bf = sb("aq_nmbf", [128, 2, NFC, 128], BF16)
            psU = self.psBig.ap()[:, 0:2]
            psV = self.psBig.ap()[:, 2:4]

            def ynn(e: Eng, cc_all=cc_all):
                e.wait(self.cc, cc_all)
                return self.sdma(e, 2, nn_rb[:],
                                 self.cc_out[NH * FDIM * HD:]
                                 .rearrange("(kv p j) -> p kv j", p=128, j=24)
                                 .rearrange("p kv (fcc h4) -> p kv fcc h4", h4=4))
            nn_tok = self.sec("Y", ynn)

            def vnn(e: Eng, nn_tok=nn_tok):
                self.swait(e, nn_tok)
                self.wconsts(e)
                t = self.vv.inc(e.tensor_tensor(
                    nn_f32[:].rearrange("p fcc (kv h4) -> p fcc kv h4", h4=4),
                    self.normT_sb[:].rearrange("p fcc (kv h4) -> p fcc kv h4", h4=4),
                    nn_rb[:].rearrange("p kv fcc h4 -> p fcc kv h4"), ALU.add))
                e.wait(self.vv, t)
                return self.vv.inc(e.tensor_copy(nn_bf[:], nn_f32[:].rearrange("p fcc h -> p h fcc")))
            nnb_tok = self.sec("V", vnn)

            self.aq_psU_free = ("se", 0)
            self.aq_psV_free_se = 0
            self.aq_mq_free = 0
            self.aq_sq_free = 0
            self.aq_base_free = 0
            self.aq_gbf_free = 0
            self.aq_memh_free = 0
            nm_ready = {}

            for h in range(NH):
                par = h % 2
                self.sec("G", lambda g: self.wloads_n(g, 2))

                def gnm(e: Eng, h=h, par=par):
                    if h >= 2:
                        e.wait(self.vv, nm_ready[h - 2])
                    e.wait(self.cc, cc_all)
                    ta = self.sdma(e, 12 + par, nmta[:, par],
                                   d["amem"][h].rearrange("(fcc p) dd -> p fcc dd", p=128))
                    tb = self.sdma(e, 14 + par, nmtb[:, par],
                                   self.cc_out[h * FDIM * HD:(h + 1) * FDIM * HD]
                                   .rearrange("(fcc p dd) -> p fcc dd", p=128, dd=128))
                    return ta, tb
                nm_toks = self.sec("G", gnm)

                def vnm(e: Eng, h=h, par=par, nm_toks=nm_toks):
                    ta, tb = nm_toks
                    self.swait(e, ta)
                    self.swait(e, tb)
                    t = self.vv.inc(e.tensor_add(nmta[:, par], nmta[:, par], nmtb[:, par]))
                    e.wait(self.vv, t)
                    t = self.vv.inc(e.tensor_copy(nm_bf[:, par], nmta[:, par]))
                    nm_ready[h] = t
                    return t
                nm_rdy = self.sec("V", vnm)

                se_pred = self.se.n
                def tq(e: Eng, h=h, se_pred=se_pred):
                    w = self.wget(e, qblk[(h, "q")]).rearrange("p (fc m) -> p fc m", m=128)
                    toks = []
                    for qh in range(2):
                        if qh == 0:
                            kind, tok = self.aq_psU_free
                            e.wait(self.se if kind == "se" else self.vv, tok)
                        else:
                            e.wait(self.se, se_pred + 1)
                        inst = None
                        for fc in range(NF):
                            for tt in range(2):
                                inst = e.matmul(psU[:, tt], w[:, fc],
                                                self.xn[:, fc, qh * HALF + tt * 512: qh * HALF + (tt + 1) * 512],
                                                start=(fc == 0), stop=(fc == NF - 1))
                        if qh == 1:
                            self.wc.inc(inst)
                            toks.append(("wc", self.wc.n))
                        else:
                            toks.append(("mm", self.mm.inc(inst)))
                    return toks
                q_toks = self.sec("T", tq)

                def sqev(e: Eng, h=h, q_toks=q_toks, se_pred=se_pred):
                    outs = []
                    for qh in range(2):
                        kind, tok = q_toks[qh]
                        e.wait(self.wc if kind == "wc" else self.mm, tok)
                        t = self.se.inc(e.activation(
                            out=qfm[:, qh * HALF:(qh + 1) * HALF], in_=ps_flatten(psU),
                            func=AF.Identity, bias=self.bq_sb[:, h:h + 1]))
                        assert t == se_pred + qh + 1, "q evac token drift"
                        outs.append(t)
                    self.aq_psU_free = ("se", outs[-1])
                    return outs
                qev = self.sec("S", sqev)

                for qh in range(2):
                    th = slice(qh * HALF, (qh + 1) * HALF)

                    def sbase2(e: Eng, qh=qh, qev=qev):
                        e.wait(self.se, qev[qh])
                        e.wait(self.vv, self.aq_base_free)
                        e.activation(out=qbase[:, 0], in_=qfm[:, th], func=AF.Relu)
                        return self.se.inc(e.activation(out=qbase[:, 1], in_=qfm[:, th],
                                                        func=AF.Relu,
                                                        scale=self.crows[:, C_NEG1:C_NEG1 + 1]))
                    qb_tok = self.sec("S", sbase2)

                    roll_pred = self.vv.n + 1
                    def troll(e: Eng, qb_tok=qb_tok, roll_pred=roll_pred):
                        e.wait(self.se, qb_tok)
                        e.wait(self.se, self.aq_psV_free_se)
                        e.wait(self.mm, self.aq_mq_free)
                        toks = []
                        for j in (1, 2, 3):
                            for cb in range(2):
                                jc = (j - 1) * 2 + cb
                                if jc >= 1:
                                    e.wait(self.vv, roll_pred + jc - 1)
                                inst = None
                                for tt in range(2):
                                    e.matmul(psV[:, tt], self.shift_bf[:, (j - 1) * 2],
                                             qbase[:, cb, tt * 512:(tt + 1) * 512], start=True, stop=False)
                                    inst = e.matmul(psV[:, tt], self.shift_bf[:, (j - 1) * 2 + 1],
                                                    qbase[:, 1 - cb, tt * 512:(tt + 1) * 512],
                                                    start=False, stop=True)
                                toks.append(self.mm.inc(inst))
                        return toks
                    rtoks = self.sec("T", troll)

                    def vroll(e: Eng, rtoks=rtoks, roll_pred=roll_pred):
                        t = 0
                        for j in (1, 2, 3):
                            for cb in range(2):
                                jc = (j - 1) * 2 + cb
                                e.wait(self.mm, rtoks[jc])
                                t = self.vv.inc(e.tensor_mul(mq[:, jc], ps_flatten(psV), qbase[:, cb]))
                                assert t == roll_pred + jc, "roll evac token drift"
                        return t
                    roll_last = self.sec("V", vroll)
                    self.aq_base_free = roll_last

                    sq_se = [0] * NFC
                    sqmm_pred = self.mm.n + 1
                    def ssq2(e: Eng, roll_pred=roll_pred, sqmm_pred=sqmm_pred, qev=qev):
                        for jc in range(NFC):
                            e.wait(self.vv, roll_pred + jc)
                            if jc == 0:
                                e.wait(self.mm, self.aq_sq_free)
                                e.wait(self.se, qev[1])
                            else:
                                e.wait(self.mm, sqmm_pred + jc - 1)
                            sq_se[jc] = self.se.inc(e.activation(out=sqq[:], in_=mq[:, jc],
                                                                 func=AF.Square))
                    self.sec("S", ssq2)

                    def tsq(e: Eng, sqmm_pred=sqmm_pred):
                        e.wait(self.vv, self.aq_memh_free)
                        toks = []
                        for jc in range(NFC):
                            e.wait(self.se, sq_se[jc])
                            inst = None
                            for tt in range(2):
                                inst = e.matmul(psU[:, tt], self.ones_bf[:], sqq[:, tt * 512:(tt + 1) * 512],
                                                start=(jc == 0), stop=(jc == NFC - 1))
                            t = self.mm.inc(inst)
                            assert t == sqmm_pred + jc, "sq mm token drift"
                            toks.append(t)
                        self.aq_sq_free = toks[-1]
                        return toks
                    sqmm = self.sec("T", tsq)

                    def tda(e: Eng, h=h, roll_last=roll_last, nnb_tok=nnb_tok):
                        e.wait(self.vv, roll_last)
                        e.wait(self.vv, nnb_tok)
                        inst = None
                        for fcc in range(NFC):
                            for tt in range(2):
                                inst = e.matmul(psV[:, tt],
                                                nn_bf[:, h, fcc:fcc + 1].broadcast_to([128, 128]),
                                                mq[:, fcc, tt * 512:(tt + 1) * 512],
                                                start=(fcc == 0), stop=(fcc == NFC - 1))
                        return self.mm.inc(inst)
                    da_mm = self.sec("T", tda)

                    def snb(e: Eng, sqmm=sqmm):
                        e.wait(self.mm, sqmm[-1])
                        return self.se.inc(e.activation(out=nbu[:], in_=ps_flatten(psU), func=AF.Sqrt))
                    nb_tok = self.sec("S", snb)

                    def vda(e: Eng, da_mm=da_mm, nb_tok=nb_tok):
                        e.wait(self.se, nb_tok)
                        t = self.vv.inc(e.tensor_scalar(nbu[:], nbu[:],
                                                        scalar1=self.crows[:, C_1E12:C_1E12 + 1],
                                                        scalar2=None, op0=ALU.max))
                        e.wait(self.vv, t)
                        e.wait(self.mm, da_mm)
                        t = self.vv.inc(e.scalar_tensor_tensor(
                            out=dau[:], in0=nbu[:], scalar=self.crows[:, C_1E8:C_1E8 + 1],
                            in1=ps_flatten(psV), op0=ALU.mult, op1=ALU.add))
                        e.wait(self.vv, t)
                        t = self.vv.inc(e.reciprocal(dau[:], dau[:]))
                        return t
                    rr_tok = self.sec("V", vda)

                    def tna(e: Eng, h=h, nb_tok=nb_tok, nm_rdy=nm_rdy):
                        e.wait(self.se, nb_tok)
                        e.wait(self.vv, nm_rdy)
                        inst = None
                        for fcc in range(NFC):
                            for tt in range(2):
                                inst = e.matmul(psU[:, tt], nm_bf[:, h % 2, fcc],
                                                mq[:, fcc, tt * 512:(tt + 1) * 512],
                                                start=(fcc == 0), stop=(fcc == NFC - 1))
                        t = self.mm.inc(inst)
                        self.aq_mq_free = t
                        return t
                    na_mm = self.sec("T", tna)

                    def vmemh(e: Eng, na_mm=na_mm, rr_tok=rr_tok):
                        e.wait(self.mm, na_mm)
                        e.wait(self.vv, rr_tok)
                        t = self.vv.inc(e.tensor_mul(memh[:], ps_flatten(psU), dau[:]))
                        self.aq_memh_free = t
                        self.aq_psU_free = ("vv", t)
                        return t
                    memh_tok = self.sec("V", vmemh)

                    def tg(e: Eng, h=h, qh=qh, rr_tok=rr_tok):
                        w = self.wget(e, qblk[(h, "g")]).rearrange("p (fc m) -> p fc m", m=128)
                        e.wait(self.vv, rr_tok)
                        inst = None
                        for fc in range(NF):
                            for tt in range(2):
                                inst = e.matmul(psV[:, tt], w[:, fc],
                                                self.xn[:, fc, qh * HALF + tt * 512: qh * HALF + (tt + 1) * 512],
                                                start=(fc == 0), stop=(fc == NF - 1))
                        if qh == 1:
                            self.wc.inc(inst)
                            return ("wc", self.wc.n)
                        return ("mm", self.mm.inc(inst))
                    g_tok = self.sec("T", tg)

                    def sg2(e: Eng, h=h, g_tok=g_tok):
                        kind, tok = g_tok
                        e.wait(self.wc if kind == "wc" else self.mm, tok)
                        e.wait(self.vv, self.aq_gbf_free)
                        t = self.se.inc(e.activation(out=gbf[:], in_=ps_flatten(psV),
                                                     func=AF.Sigmoid, bias=self.bg_sb[:, h:h + 1]))
                        self.aq_psV_free_se = t
                        return t
                    gb_tok = self.sec("S", sg2)

                    def vfin(e: Eng, h=h, th=th, gb_tok=gb_tok, memh_tok=memh_tok):
                        e.wait(self.se, gb_tok)
                        e.wait(self.vv, memh_tok)
                        t = self.vv.inc(e.tensor_mul(t3b[:], gbf[:], memh[:]))
                        self.aq_gbf_free = t
                        e.wait(self.vv, t)
                        return self.vv.inc(e.tensor_add(self.delta[:, h, th], t3b[:], self.xn[:, h, th]))
                    self.sec("V", vfin)
        self.join()

    def emit_transposes_after_se(self, pairs, psT, se_tok, dst_guard=None):
        return self._emit_transposes(pairs, psT, ("se", se_tok), dst_guard)

    def emit_transposes_after_vv(self, pairs, psT, vv_tok, dst_guard=None):
        return self._emit_transposes(pairs, psT, ("vv", vv_tok), dst_guard)

    def _emit_transposes(self, pairs, psT, gate, dst_guard=None):
        groups = [pairs[i:i + 4] for i in range(0, len(pairs), 4)]
        pred = self.vv.n

        def tsec(e: Eng):
            toks = []
            kind, gtok = gate
            e.wait(self.se if kind == "se" else self.vv, gtok)
            for g, grp in enumerate(groups):
                if g == 0:
                    e.wait(self.vv, self.aa_psT_free)
                else:
                    e.wait(self.vv, pred + g)
                inst = None
                for k, (s_ap, _) in enumerate(grp):
                    inst = e.transpose(psT[:, k * 128:(k + 1) * 128], s_ap, self.ident_bf[:])
                toks.append(self.mm.inc(inst))
            return toks
        toks = self.sec("T", tsec)

        def vsec(e: Eng):
            t = 0
            if dst_guard is not None:
                kind, tok = dst_guard
                e.wait({"se": self.se, "mm": self.mm, "vv": self.vv}[kind], tok)
            for g, grp in enumerate(groups):
                e.wait(self.mm, toks[g])
                inst = None
                for k, (_, d_ap) in enumerate(grp):
                    inst = e.tensor_copy(d_ap, psT[:, k * 128:(k + 1) * 128])
                t = self.vv.inc(inst)
                assert t == pred + g + 1, "transpose evac token drift"
            self.aa_psT_free = t
            return t
        return self.sec("V", vsec)

    def emit_attn_b(self):
        nc = self.nc
        d = self.d
        self.uid += 1
        u = self.uid

        g1blk = self.wadd(d["bWg1"].rearrange("(fc p) m -> p fc m", p=128))
        hblk = {}
        for h in range(NH):
            for nm, wn in (("q", "bWq"), ("k", "bWk"), ("v", "bWv")):
                hblk[(h, nm)] = self.wadd(
                    d[wn][:, h * 128:(h + 1) * 128].rearrange("(fc p) m -> p fc m", p=128))
        woblk = {}
        for dc in range(NF):
            woblk[dc] = self.wadd(
                d["bWo"][:, dc * 128:(dc + 1) * 128].rearrange("(fc p) m -> p fc m", p=128))

        with contextlib.ExitStack() as ps:
            sb = lambda name, shape, dt: ps.enter_context(nc.sbuf_tensor(f"{name}_{u}", shape, dt))
            g1fm = sb("ab_g1", [128, S], BF16)
            wg2 = sb("ab_wg2", [128, S], BF16)
            qfm = sb("ab_qfm", [128, S], BF16)
            kfm = sb("ab_kfm", [128, S], BF16)
            fmst = sb("ab_fm", [128, S], BF16)
            ktm = sb("ab_ktm", [128, NTC, 128], BF16)
            vtm = sb("ab_vtm", [128, NTC, 128], BF16)
            Ssb = sb("ab_S", [128, 128], F32)
            Sbf = sb("ab_Sbf", [128, 128], BF16)
            DT = sb("ab_DT", [128, 128], F32)
            scT = sb("ab_scT", [128, 2, 128], BF16)
            kkd = sb("ab_kkd", [128, 2, 128], BF16)
            octm = sb("ab_oc", [128, 128], F32)
            o_tm = sb("ab_otm", [128, NTC, 128], BF16)
            sg = sb("ab_sg", [128, S], BF16)
            og = sb("ab_og", [128, S], BF16)
            sqf = sb("ab_sqf", [128, S], BF16)
            ssr = sb("ab_ssr", [128, NTC], F32)
            rf = sb("ab_rf", [128, NTC], F32)
            onw = sb("ab_onw", [128, S], BF16)
            qd = sb("ab_qd", [128, NH], F32)
            kd = sb("ab_kd", [128, NH], F32)
            cd = sb("ab_cd", [128, NH], F32)
            psP = self.psBig.ap()
            psT = self.psBig2.ap()[:, 0].bitcast(BF16)[:, 0:512]
            psS2 = self.psBig2.ap()[:, 1, 0:128]
            psO = self.psBig2.ap()[:, 2, 0:256]
            psU2 = self.psBig2.ap()[:, 3, 0:128]

            self.ab_psT_free = 0
            self.aa_psT_free = 0

            def gw2(e: Eng):
                return self.sdma(e, 12, wg2[:], d["bWg2"])
            wg2_tok = self.sec("G", gw2)

            def tld(e: Eng):
                self.wconsts(e)
                return self.mm.inc(e.matmul(psS2[:, 0:NH], self.onesf_row[:], self.ld_row[:],
                                            start=True, stop=True))
            ld_mm = self.sec("T", tld)

            def vld(e: Eng, ld_mm=ld_mm):
                e.wait(self.mm, ld_mm)
                return self.vv.inc(e.tensor_copy(self.ldb[:], psS2[:, 0:NH]))
            ldb_tok = self.sec("V", vld)

            def sdec(e: Eng, ldb_tok=ldb_tok):
                e.wait(self.vv, ldb_tok)
                t = 0
                for h in range(NH):
                    e.activation(out=qd[:, h:h + 1], in_=self.crows[:, C_IP1:C_IP1 + 1],
                                 func=AF.Exp, scale=self.ldb[:, h:h + 1])
                    e.activation(out=kd[:, h:h + 1], in_=self.crows[:, C_127M:C_127M + 1],
                                 func=AF.Exp, scale=self.ldb[:, h:h + 1])
                    t = self.se.inc(e.activation(out=cd[:, h:h + 1], in_=self.crows[:, C_128:C_128 + 1],
                                                 func=AF.Exp, scale=self.ldb[:, h:h + 1]))
                return t
            dec_tok = self.sec("S", sdec)

            self.sec("G", lambda g: self.wloads_n(g, 1))
            def tg1(e: Eng):
                w = self.wget(e, g1blk).rearrange("p (fc m) -> p fc m", m=128)
                inst = None
                for fc in range(NF):
                    for tt in range(4):
                        inst = e.matmul(psP[:, tt], w[:, fc],
                                        self.xn[:, fc, tt * 512:(tt + 1) * 512],
                                        start=(fc == 0), stop=(fc == NF - 1))
                self.wc.inc(inst)
                return self.wc.n
            g1_mm = self.sec("T", tg1)

            def sg1(e: Eng, g1_mm=g1_mm):
                e.wait(self.wc, g1_mm)
                return self.se.inc(e.activation(out=g1fm[:], in_=ps_flatten(psP), func=AF.Copy))
            g1_tok = self.sec("S", sg1)
            self.ab_psP_free = ("se", g1_tok)

            onw_store = {}
            self.ab_fm_free = 0
            self.ab_qfm_free = 0
            self.ab_ktm_rd = 0
            self.ab_kfm_rd = 0
            self.ab_og_rd = 0
            self.ab_sg_free = 0
            self.ab_dt_free = 0
            self.ab_sxx_rd = 0

            for h in range(NH):
                self.sec("G", lambda g: self.wloads_n(g, 3))

                def tq(e: Eng, h=h):
                    w = self.wget(e, hblk[(h, "q")]).rearrange("p (fc m) -> p fc m", m=128)
                    kind, tok = self.ab_psP_free
                    e.wait(self.se if kind == "se" else self.vv, tok)
                    inst = None
                    for fc in range(NF):
                        for tt in range(4):
                            inst = e.matmul(psP[:, tt], w[:, fc],
                                            self.xn[:, fc, tt * 512:(tt + 1) * 512],
                                            start=(fc == 0), stop=(fc == NF - 1))
                    self.wc.inc(inst)
                    return self.wc.n
                q_mm = self.sec("T", tq)

                def sq(e: Eng, q_mm=q_mm):
                    e.wait(self.wc, q_mm)
                    e.wait(self.mm, self.ab_qfm_free)
                    t = self.se.inc(e.activation(out=qfm[:], in_=ps_flatten(psP), func=AF.Silu))
                    self.ab_psP_free = ("se", t)
                    return t
                q_tok = self.sec("S", sq)

                tm_toks = {}
                kfm_tok = [0]
                for nm, tmbuf, fbuf in (("k", ktm, kfm), ("v", vtm, fmst)):
                    def tkv(e: Eng, h=h, nm=nm):
                        w = self.wget(e, hblk[(h, nm)]).rearrange("p (fc m) -> p fc m", m=128)
                        kind, tok = self.ab_psP_free
                        e.wait(self.se if kind == "se" else self.vv, tok)
                        inst = None
                        for fc in range(NF):
                            for tt in range(4):
                                inst = e.matmul(psP[:, tt], w[:, fc],
                                                self.xn[:, fc, tt * 512:(tt + 1) * 512],
                                                start=(fc == 0), stop=(fc == NF - 1))
                        self.wc.inc(inst)
                        return self.wc.n
                    kv_mm = self.sec("T", tkv)

                    def skv(e: Eng, kv_mm=kv_mm, nm=nm, fbuf=fbuf):
                        e.wait(self.wc, kv_mm)
                        if nm == "k":
                            e.wait(self.mm, self.ab_kfm_rd)
                        else:
                            e.wait(self.vv, self.ab_fm_free)
                        t = self.se.inc(e.activation(out=fbuf[:], in_=ps_flatten(psP), func=AF.Silu))
                        self.ab_psP_free = ("se", t)
                        if nm == "k":
                            kfm_tok[0] = t
                        return t
                    kv_tok = self.sec("S", skv)

                    prs = [(fbuf[:, tc * 128:(tc + 1) * 128], tmbuf[:, tc]) for tc in range(NTC)]
                    tm_toks[nm] = self.emit_transposes_after_se(prs, psT, kv_tok,
                                                               dst_guard=("mm", self.ab_ktm_rd))
                    if nm == "v":
                        self.ab_fm_free = tm_toks[nm]

                def sdt(e: Eng, h=h, dec_tok=dec_tok):
                    e.wait(self.se, dec_tok)
                    e.wait(self.vv, self.ab_dt_free)
                    return self.se.inc(e.activation(out=DT[:], in_=self.diffT_sb[:],
                                                    func=AF.Exp, scale=self.ldb[:, h:h + 1]))
                dt_tok = self.sec("S", sdt)

                def vz(e: Eng):
                    e.wait(self.mm, self.ab_sxx_rd)
                    t = self.vv.inc(e.memset(Ssb[:], 0.0))
                    t = self.vv.inc(e.memset(Sbf[:], 0.0))
                    return t
                s0_tok = self.sec("V", vz)

                sc_ev = [0] * NTC
                kk_ev = [0] * NTC
                oc_ev = [0] * NTC
                ot_ev = [0] * NTC
                sbf_ev = [0] * NTC
                sc_mm = [0] * NTC
                oi_mm = [0] * NTC
                oe_mm = [0] * NTC
                su_mm = [0] * NTC
                for c in range(NTC):
                    cs = slice(c * 128, (c + 1) * 128)

                    def tsc(e: Eng, c=c, cs=cs, q_tok=q_tok, ktok=kfm_tok[0]):
                        e.wait(self.se, q_tok)
                        e.wait(self.se, ktok)
                        if c >= 1:
                            e.wait(self.vv, sc_ev[c - 1])
                        sc_mm[c] = self.mm.inc(e.matmul(psS2, kfm[:, cs], qfm[:, cs],
                                                        start=True, stop=True))
                    self.sec("T", tsc)

                    def vsc(e: Eng, c=c, dt_tok=dt_tok):
                        e.wait(self.mm, sc_mm[c])
                        e.wait(self.se, dt_tok)
                        if c >= 2:
                            e.wait(self.mm, oi_mm[c - 2])
                        sc_ev[c] = self.vv.inc(e.tensor_mul(scT[:, c % 2], psS2, DT[:]))
                    self.sec("V", vsc)

                    def vkk(e: Eng, c=c, h=h, ktok=tm_toks["k"], dec_tok=dec_tok):
                        e.wait(self.vv, ktok)
                        e.wait(self.se, dec_tok)
                        if c >= 2:
                            e.wait(self.mm, su_mm[c - 2])
                        kk_ev[c] = self.vv.inc(e.tensor_scalar(
                            kkd[:, c % 2], ktm[:, c], scalar1=kd[:, h:h + 1], scalar2=None, op0=ALU.mult))
                    self.sec("V", vkk)

                    def toi(e: Eng, c=c, cs=cs, vtok=tm_toks["v"], q_tok=q_tok):
                        e.wait(self.vv, sc_ev[c])
                        e.wait(self.vv, vtok)
                        if c >= 1:
                            e.wait(self.vv, oc_ev[c - 1])
                            e.wait(self.vv, ot_ev[c - 1])
                        oi_mm[c] = self.mm.inc(e.matmul(psO[:, 0:128], scT[:, c % 2], vtm[:, c],
                                                        start=True, stop=True))
                        e.wait(self.vv, sbf_ev[c - 1] if c >= 1 else s0_tok)
                        oe_mm[c] = self.mm.inc(e.matmul(psO[:, 128:256], qfm[:, cs], Sbf[:],
                                                        start=True, stop=True))
                        e.wait(self.vv, kk_ev[c])
                        if c >= 1:
                            e.wait(self.vv, sbf_ev[c - 1])
                        su_mm[c] = self.mm.inc(e.matmul(psU2, kkd[:, c % 2], vtm[:, c],
                                                        start=True, stop=True))
                    self.sec("T", toi)

                    def vo(e: Eng, c=c, h=h):
                        e.wait(self.mm, oi_mm[c])
                        e.wait(self.mm, oe_mm[c])
                        e.wait(self.mm, su_mm[c])
                        oc_ev[c] = self.vv.inc(e.tensor_copy(octm[:], psO[:, 0:128]))
                        e.wait(self.mm, oe_mm[c])
                        e.wait(self.vv, oc_ev[c])
                        ot_ev[c] = self.vv.inc(e.scalar_tensor_tensor(
                            out=o_tm[:, c], in0=psO[:, 128:256], scalar=qd[:, h:h + 1],
                            in1=octm[:], op0=ALU.mult, op1=ALU.add))
                        e.wait(self.mm, su_mm[c])
                        t = self.vv.inc(e.scalar_tensor_tensor(
                            out=Ssb[:], in0=Ssb[:], scalar=cd[:, h:h + 1], in1=psU2,
                            op0=ALU.mult, op1=ALU.add))
                        e.wait(self.vv, t)
                        sbf_ev[c] = self.vv.inc(e.tensor_copy(Sbf[:], Ssb[:]))
                    self.sec("V", vo)
                self.ab_sxx_rd = max(oe_mm[NTC - 1], su_mm[NTC - 1])
                self.ab_ktm_rd = su_mm[NTC - 1]
                self.ab_kfm_rd = sc_mm[NTC - 1]
                self.ab_qfm_free = oe_mm[NTC - 1]
                self.ab_dt_free = sc_ev[NTC - 1]

                def tg2(e: Eng, h=h, g1_tok=g1_tok, wg2_tok=wg2_tok):
                    e.wait(self.se, g1_tok)
                    self.swait(e, wg2_tok)
                    kind, tok = self.ab_psP_free
                    e.wait(self.se if kind == "se" else self.vv, tok)
                    inst = None
                    for tc in range(NTC):
                        inst = e.matmul(psP.rearrange("p a b -> p (a b)")[:, tc * 128:(tc + 1) * 128],
                                        g1fm[:, tc * 128:(tc + 1) * 128],
                                        wg2[:, h * 128:(h + 1) * 128], start=True, stop=True)
                    return self.mm.inc(inst)
                g2_mm = self.sec("T", tg2)

                def ssg(e: Eng, g2_mm=g2_mm):
                    e.wait(self.mm, g2_mm)
                    e.wait(self.vv, self.ab_sg_free)
                    t = self.se.inc(e.activation(out=sg[:], in_=ps_flatten(psP), func=AF.Sigmoid))
                    self.ab_psP_free = ("se", t)
                    return t
                sg_tok = self.sec("S", ssg)

                def vog(e: Eng, h=h, sg_tok=sg_tok):
                    e.wait(self.se, sg_tok)
                    e.wait(self.vv, ot_ev[NTC - 1])
                    e.wait(self.se, self.ab_og_rd)
                    t = self.vv.inc(e.tensor_mul(og[:], o_tm[:].rearrange("p a b -> p (a b)"), sg[:]))
                    self.ab_sg_free = t
                    return t
                og_tok = self.sec("V", vog)

                def ssq3(e: Eng, og_tok=og_tok):
                    e.wait(self.vv, og_tok)
                    t = self.se.inc(e.activation(out=sqf[:], in_=og[:], func=AF.Square))
                    self.ab_og_rd = t
                    return t
                sq3_tok = self.sec("S", ssq3)

                def vssr(e: Eng, sq3_tok=sq3_tok):
                    e.wait(self.se, sq3_tok)
                    t = self.vv.inc(e.tensor_reduce(ssr[:], sqf[:].rearrange("p (a b) -> p a b", b=128),
                                                    axis=AX.X, op=ALU.add))
                    e.wait(self.vv, t)
                    t = self.vv.inc(e.tensor_scalar(ssr[:], ssr[:], scalar1=1.0 / HD, scalar2=None,
                                                    op0=ALU.mult))
                    return t
                ssr_tok = self.sec("V", vssr)

                def ssrt(e: Eng, ssr_tok=ssr_tok):
                    e.wait(self.vv, ssr_tok)
                    return self.se.inc(e.activation(out=ssr[:], in_=ssr[:], func=AF.Sqrt,
                                                    bias=self.crows[:, C_LNEPS:C_LNEPS + 1]))
                srt_tok = self.sec("S", ssrt)

                def vrf(e: Eng, srt_tok=srt_tok):
                    e.wait(self.se, srt_tok)
                    t = self.vv.inc(e.reciprocal(rf[:], ssr[:]))
                    e.wait(self.vv, t)
                    last = 0
                    for tc in range(NTC):
                        last = self.vv.inc(e.tensor_scalar(
                            og[:, tc * 128:(tc + 1) * 128], og[:, tc * 128:(tc + 1) * 128],
                            scalar1=rf[:, tc:tc + 1], scalar2=None, op0=ALU.mult))
                    return last
                on_tok = self.sec("V", vrf)

                groups = [[(og[:, (g * 4 + k) * 128:(g * 4 + k + 1) * 128],
                            onw[:, (g * 4 + k) * 128:(g * 4 + k + 1) * 128]) for k in range(4)]
                          for g in range(4)]
                tr_pred = self.vv.n

                def ttro(e: Eng, on_tok=on_tok, tr_pred=tr_pred):
                    toks = []
                    for g, grp in enumerate(groups):
                        e.wait(self.vv, on_tok)
                        if g == 0:
                            e.wait(self.vv, self.aa_psT_free)
                        else:
                            e.wait(self.vv, tr_pred + g)
                        inst = None
                        for k2, (s_ap, _) in enumerate(grp):
                            inst = e.transpose(psT[:, k2 * 128:(k2 + 1) * 128], s_ap, self.ident_bf[:])
                        toks.append(self.mm.inc(inst))
                    return toks
                trtoks = self.sec("T", ttro)

                def vtro(e: Eng, h=h, trtoks=trtoks, tr_pred=tr_pred):
                    t = 0
                    if h >= 1:
                        self.swait(e, onw_store[h - 1])
                    for g, grp in enumerate(groups):
                        e.wait(self.mm, trtoks[g])
                        t = self.vv.inc(e.tensor_scalar(
                            onw[:, g * 512:(g + 1) * 512], psT[:],
                            scalar1=self.bnw_sb[:, h:h + 1], scalar2=None, op0=ALU.mult))
                        assert t == tr_pred + g + 1, "onw transpose token drift"
                    self.aa_psT_free = t
                    return t
                onw_tok = self.sec("V", vtro)

                def yst(e: Eng, h=h, onw_tok=onw_tok):
                    e.wait(self.vv, onw_tok)
                    if h >= 1:
                        self.swait(e, onw_store[h - 1])
                    onw_store[h] = self.sdma(e, 4, self.onw_scr[h * 128:(h + 1) * 128, :], onw[:])
                self.sec("Y", yst)

            rb_tok = [None] * NF
            last_xn_rd = self.wc.n
            for ec in range(NF):
                def yrb(e: Eng, ec=ec):
                    e.wait(self.wc, last_xn_rd)
                    self.swait(e, onw_store[NH - 1])
                    if ec >= 2:
                        self.swait(e, rb_tok[ec - 2])
                    rb_tok[ec] = self.sdma(e, ec % 2, self.xn[:, ec],
                                           self.onw_scr[ec * 128:(ec + 1) * 128, :])
                self.sec("Y", yrb)

            self.sec("G", lambda g: self.wloads_n(g, NF))
            wo_pred_v = self.vv.n
            for dc in range(NF):
                def two2(e: Eng, dc=dc):
                    w = self.wget(e, woblk[dc]).rearrange("p (fc m) -> p fc m", m=128)
                    if dc == 0:
                        kind, tok = self.ab_psP_free
                        e.wait(self.se if kind == "se" else self.vv, tok)
                    else:
                        e.wait(self.vv, wo_pred_v + dc)
                    inst = None
                    for ec in range(NF):
                        self.swait(e, rb_tok[ec])
                        for tt in range(4):
                            inst = e.matmul(psP[:, tt], w[:, ec],
                                            self.xn[:, ec, tt * 512:(tt + 1) * 512],
                                            start=(ec == 0), stop=(ec == NF - 1))
                    self.wc.inc(inst)
                    return self.wc.n
                wo_mm = self.sec("T", two2)

                def vwo2(e: Eng, dc=dc, wo_mm=wo_mm):
                    e.wait(self.wc, wo_mm)
                    t = self.vv.inc(e.tensor_copy(self.delta[:, dc], ps_flatten(psP)))
                    assert t == wo_pred_v + dc + 1, "wo evac token drift"
                    return t
                self.sec("V", vwo2)
        self.join()

    def finish(self):
        def yf(e: Eng):
            for s in self.dstr:
                e.wait(s, s.n)
        self.sec("Y", yf)
        self.stk.close()
        return self.nc


def build_program(parts=("attn_a", "ffn1", "attn_b", "ffn2"), debug_out=()):
    k = K(debug_out=debug_out)
    k.emit_consts()
    srcs = k.d["x0"]
    if not parts:
        k.emit_boundary(srcs, k.out_d, has_delta=False, write_xn=True)
        k.emit_dump("dbg_xn0", "xn")
        return k.finish()
    k.emit_boundary(srcs, None, has_delta=False, write_xn=True)
    k.emit_dump("dbg_xn0", "xn")
    for i, p in enumerate(parts):
        last = i == len(parts) - 1
        if p == "attn_a":
            k.emit_attn_a()
        elif p == "attn_b":
            k.emit_attn_b()
        elif p.startswith("ffn"):
            k.emit_ffn()
        else:
            raise ValueError(p)
        k.emit_dump(f"dbg_delta{i}", "delta")
        if last:
            k.emit_boundary(srcs, k.out_d, has_delta=True, write_xn=False)
        else:
            k.emit_boundary(srcs, k.x_scr, has_delta=True, write_xn=True)
            srcs = k.x_scr
    return k.finish()


def _prep_inputs(inputs):
    a = lambda x: np.ascontiguousarray(np.asarray(x, dtype=np.float32))
    hs = a(inputs["L0_attn_a_hidden_states"])
    chunkvec = lambda v, n: np.ascontiguousarray(a(v).reshape(n, 128).T)
    common = {
        "aWqT": a(np.asarray(inputs["L0_attn_a_Wq"]).T), "aWkT": a(np.asarray(inputs["L0_attn_a_Wk"]).T),
        "aWvT": a(np.asarray(inputs["L0_attn_a_Wv"]).T), "aWgT": a(np.asarray(inputs["L0_attn_a_Wg"]).T),
        "aWmbT": a(np.asarray(inputs["L0_attn_a_Wmb"]).T),
        "abq": chunkvec(inputs["L0_attn_a_bq"], NF), "abk": chunkvec(inputs["L0_attn_a_bk"], NKV),
        "abv": chunkvec(inputs["L0_attn_a_bv"], NKV), "abg": chunkvec(inputs["L0_attn_a_bg"], NF),
        "abmb": chunkvec(inputs["L0_attn_a_bmb"], NF),
        "amem": a(inputs["L0_attn_a_memory"]),
        "anormT": np.ascontiguousarray(
            a(inputs["L0_attn_a_norm"]).T.reshape(NFC, 128, NH).transpose(1, 0, 2)),
        "bld": a(inputs["L0_attn_b_log_decay"]),
        "bWq": a(inputs["L0_attn_b_Wq"]), "bWk": a(inputs["L0_attn_b_Wk"]),
        "bWv": a(inputs["L0_attn_b_Wv"]), "bWo": a(inputs["L0_attn_b_Wo"]),
        "bWg1": a(inputs["L0_attn_b_Wg1"]), "bWg2": a(inputs["L0_attn_b_Wg2"]),
        "bnw": chunkvec(inputs["L0_attn_b_norm_w"], NF),
        "fgk": np.ascontiguousarray(
            a(inputs["L0_ffn_gate_kernel"]).reshape(NF, 128, NEXP).transpose(1, 0, 2)),
        "fwi0": a(inputs["L0_ffn_wi_0"]), "fwi1": a(inputs["L0_ffn_wi_1"]),
        "fwo": a(inputs["L0_ffn_wo"]),
    }
    in_maps = []
    for b in range(B):
        m = dict(common)
        m["x0"] = np.ascontiguousarray(hs[b].T)
        in_maps.append(m)
    return in_maps


def kernel(**inputs):
    in_maps = _prep_inputs(inputs)
    nc = build_program()
    res = run_bass_kernel_spmd(nc, in_maps, core_ids=list(range(B)))
    out = np.stack([np.asarray(res.results[b]["out"]).T for b in range(B)], axis=0)
    return out.astype(np.float32)


# version 04 (final)
# speedup vs baseline: 1.0736x; 1.0736x over previous
import contextlib
import os
import numpy as np
import ml_dtypes
import concourse.bass as bass
import concourse.mybir as mybir
from concourse.bass_utils import run_bass_kernel_spmd

F32 = mybir.dt.float32
BF16 = mybir.dt.bfloat16
AF = mybir.ActivationFunctionType
ALU = mybir.AluOpType
AX = mybir.AxisListType

B, S, D = 8, 2048, 2048
NF = D // 128
NTC = S // 128
HALF = S // 2
NKV, NH, HD = 4, 16, 128
FDIM, NFC = 768, 6
NEXP = 8
LN_EPS = 1e-5
RMS_EPS = 1e-5
NBUF = 4

bf16np = ml_dtypes.bfloat16

C_IP1, C_127M, C_128, C_N1E30, C_1E8, C_1E12, C_ONE, C_LNEPS, C_NEG1 = range(9)


def ps_flatten(ps):
    return ps.rearrange("p a b -> p (a b)")


def _shift_mats():
    mats = np.zeros((3, 2, 128, 128), dtype=np.float32)
    for j in (1, 2, 3):
        for m in range(128):
            if m - j >= 0:
                mats[j - 1, 0, m - j, m] = 1.0
            else:
                mats[j - 1, 1, 128 + m - j, m] = 1.0
    return mats


class Sem:
    def __init__(self, nc, stk, name):
        self.h = stk.enter_context(nc.semaphore(name))
        self.n = 0

    def inc(self, inst, k=1):
        inst.then_inc(self.h, k)
        self.n += k
        return self.n


class Eng:
    def __init__(self, eng, name, waited):
        self.e = eng
        self.name = name
        self.waited = waited

    def wait(self, sem, val):
        if val is None or val <= 0:
            return
        key = id(sem)
        if self.waited.get(key, 0) >= val:
            return
        self.e.wait_ge(sem.h, val)
        self.waited[key] = val

    def __getattr__(self, a):
        return getattr(self.e, a)


class K:

    def __init__(self, debug_out=()):
        nc = bass.Bass("TRN2", target_bir_lowering=False)
        self.nc = nc
        self.stk = contextlib.ExitStack()
        self.debug_out = set(debug_out)
        stk = self.stk

        dd = {}
        def di(name, shape, dt=F32):
            dd[name] = nc.dram_tensor(name, shape, dt, kind="ExternalInput").ap()
        di("x0", [D, S])
        di("aWqT", [D, D]); di("aWkT", [D, 512]); di("aWvT", [D, 512])
        di("aWgT", [D, D]); di("aWmbT", [D, D])
        di("abq", [128, NF]); di("abk", [128, NKV]); di("abv", [128, NKV])
        di("abg", [128, NF]); di("abmb", [128, NF])
        di("amem", [NH, FDIM, HD]); di("anormT", [128, NFC, NH])
        di("bld", [NH])
        di("bWq", [D, D]); di("bWk", [D, D]); di("bWv", [D, D]); di("bWo", [D, D])
        di("bWg1", [D, HD]); di("bWg2", [HD, D]); di("bnw", [128, NF])
        di("fgk", [128, NF, NEXP])
        di("fwi0", [NEXP, D, D]); di("fwi1", [NEXP, D, D]); di("fwo", [NEXP, D, D])
        self.d = dd
        self.out_d = nc.dram_tensor("out", [D, S], F32, kind="ExternalOutput").ap()
        self.x_scr = nc.dram_tensor("x_scr", [D, S], F32, kind="Internal").ap()
        self.onw_scr = nc.dram_tensor("onw_scr", [D, S], BF16, kind="Internal").ap()
        self.CCN = NH * FDIM * HD + NKV * 128 * 24
        self.cc_in = nc.dram_tensor("cc_in", [self.CCN], F32, kind="Internal").ap()
        self.cc_out = nc.dram_tensor("cc_out", [self.CCN], F32, kind="Internal").ap()
        self.dbg = {}
        for name in self.debug_out:
            self.dbg[name] = nc.dram_tensor(name, [128, NF, S], F32, kind="ExternalOutput").ap()

        self.c_ones_bf = nc.inline_tensor(np.ones((128, 128), dtype=bf16np), name="c_ones_bf")
        self.c_ident_bf = nc.inline_tensor(np.eye(128, dtype=bf16np), name="c_ident_bf")
        self.c_shift = nc.inline_tensor(
            np.ascontiguousarray(_shift_mats().transpose(2, 0, 1, 3).reshape(128, 6, 128)).astype(bf16np),
            name="c_shift")
        ii, jj = np.mgrid[0:128, 0:128]
        diffT = np.where(jj >= ii, (jj - ii).astype(np.float32), np.float32(1e30))
        self.c_diffT = nc.inline_tensor(diffT.astype(np.float32), name="c_diffT")
        rows = np.arange(128, dtype=np.float32)[:, None]
        crows = np.concatenate(
            [rows + 1.0, 127.0 - rows, np.full((128, 1), 128.0, np.float32),
             np.full((128, 1), -1e30, np.float32), np.full((128, 1), 1e-8, np.float32),
             np.full((128, 1), 1e-12, np.float32), np.full((128, 1), 1.0, np.float32),
             np.full((128, 1), LN_EPS, np.float32), np.full((128, 1), -1.0, np.float32)],
            axis=1).astype(np.float32)
        self.c_crows = nc.inline_tensor(crows, name="c_crows")
        self.c_onesf = nc.inline_tensor(np.ones((1, 128), dtype=np.float32), name="c_onesf")

        sb = lambda name, shape, dt: stk.enter_context(nc.sbuf_tensor(name, shape, dt))
        self.xn = sb("sb_xn", [128, NF, S], BF16)
        self.delta = sb("sb_delta", [128, NF, S], BF16)
        self.wring = sb("sb_wring", [128, NBUF, 2048], BF16)
        self.ones_bf = sb("sb_ones", [128, 128], BF16)
        self.ident_bf = sb("sb_ident", [128, 128], BF16)
        self.shift_bf = sb("sb_shift", [128, 6, 128], BF16)
        self.diffT_sb = sb("sb_diffT", [128, 128], F32)
        self.crows = sb("sb_crows", [128, 9], F32)
        self.onesf_row = sb("sb_onesf", [1, 128], F32)
        self.bq_sb = sb("sb_bq", [128, NF], F32)
        self.bk_sb = sb("sb_bk", [128, NKV], F32)
        self.bv_sb = sb("sb_bv", [128, NKV], F32)
        self.bg_sb = sb("sb_bg", [128, NF], F32)
        self.bmb_sb = sb("sb_bmb", [128, NF], F32)
        self.normT_sb = sb("sb_normT", [128, NFC, NH], F32)
        self.normT_bf = sb("sb_normTb", [128, NFC, NH], BF16)
        self.bnw_sb = sb("sb_bnw", [128, NF], F32)
        self.fgk_bf = sb("sb_fgk", [128, NF, NEXP], BF16)
        self.ld_row = sb("sb_ldrow", [1, NH], F32)
        self.ldb = sb("sb_ldb", [128, NH], F32)
        self.dumpb = sb("sb_dumpb", [128, 768], BF16)
        self.jdump = sb("sb_jdump", [128, 8], BF16)
        self.psBig = stk.enter_context(nc.psum_tensor("ps_big", [128, 4, 512], F32))
        self.psBig2 = stk.enter_context(nc.psum_tensor("ps_big2", [128, 4, 512], F32))

        mk = lambda name: Sem(nc, stk, name)
        self.dstr = [mk(f"s_d{i}") for i in range(16)]
        self.wc = mk("s_wc")
        self.mm = mk("s_mm")
        self.vv = mk("s_vv")
        self.se = mk("s_se")
        self.ge = mk("s_ge")
        self.cc = mk("s_cc")
        self.sst = self.dstr[4]
        self.joins = [mk(f"s_j{i}") for i in range(5)]
        self.join_n = 0
        self.const_tok = 0

        self.uid = 0
        self.wplan = []
        self.wemit = 0
        self.wtok = []
        self.aa_psT_free = 0
        self.waited = {w: {} for w in ("T", "V", "S", "G", "Y")}

        self.block = stk.enter_context(nc.Block())
        bl = self.block
        self._mk = {"T": bl.tensor, "V": bl.vector, "S": bl.scalar, "G": bl.gpsimd, "Y": bl.sync}

    def sdma(self, e, sidx, out, in_):
        s = self.dstr[sidx]
        tok = s.inc(e.dma_start(out=out, in_=in_), 16)
        return (sidx, tok)

    def sdma_accum(self, e, sidx, out, in_, accum):
        s = self.dstr[sidx]
        if accum:
            tok = s.inc(e.dma_start(out=out, in_=in_, accum_op=ALU.add), 16)
        else:
            tok = s.inc(e.dma_start(out=out, in_=in_), 16)
        return (sidx, tok)

    def swait(self, e, tok):
        if tok is None:
            return
        e.wait(self.dstr[tok[0]], tok[1])

    def sec(self, which, fn):
        holder = {}
        def run(e):
            holder["r"] = fn(Eng(e, which, self.waited[which]))
        self._mk[which](run)
        return holder.get("r")

    def wadd(self, src_ap):
        self.wplan.append(src_ap)
        return len(self.wplan) - 1

    def wloads_n(self, g: Eng, n):
        stop = min(self.wemit + n, len(self.wplan))
        while self.wemit < stop:
            i = self.wemit
            src = self.wplan[i]
            if i >= NBUF:
                g.wait(self.wc, i - NBUF + 1)
            nn = src.size() // 128
            dst = self.wring[:, i % NBUF, :nn]
            self.wtok.append(self.sdma(g, 6 + i % NBUF, dst, src))
            self.wemit += 1

    def wloads(self, g: Eng):
        while self.wemit < len(self.wplan):
            i = self.wemit
            src = self.wplan[i]
            if i >= NBUF:
                g.wait(self.wc, i - NBUF + 1)
            n = src.size() // 128
            dst = self.wring[:, i % NBUF, :n]
            self.wtok.append(self.sdma(g, 6 + i % NBUF, dst, src))
            self.wemit += 1

    def wget(self, t: Eng, i):
        self.swait(t, self.wtok[i])
        return self.wring[:, i % NBUF]

    def wdone(self, t: Eng):
        self.wc.inc(t.nop())

    def join(self):
        self.join_n += 1
        n = self.join_n
        for idx, which in enumerate(("T", "V", "S", "G", "Y")):
            j = self.joins[idx]
            def fin(e, j=j, which=which):
                if which == "T":
                    e.wait(self.wc, self.wc.n)
                    e.wait(self.mm, self.mm.n)
                    j.inc(e.nop())
                elif which == "Y":
                    for s in self.dstr:
                        e.wait(s, s.n)
                    j.inc(e.nop())
                elif which == "S":
                    j.inc(e.copy(self.jdump[0:1, 2:3], self.jdump[0:1, 4:5]))
                elif which == "V":
                    j.inc(e.memset(self.jdump[0:1, 0:1], 0.0))
                else:
                    j.inc(e.memset(self.jdump[0:1, 1:2], 0.0))
            self.sec(which, fin)
        for which in ("T", "V", "S", "G", "Y"):
            def wt(e):
                for j in self.joins:
                    e.wait(j, n)
            self.sec(which, wt)

    def emit_consts(self):
        d = self.d
        def y(e: Eng):
            dma = lambda dst, src: self.dstr[10].inc(e.dma_start(out=dst, in_=src), 16)
            dma(self.ones_bf[:], self.c_ones_bf.ap())
            dma(self.ident_bf[:], self.c_ident_bf.ap())
            dma(self.shift_bf[:], self.c_shift.ap())
            dma(self.diffT_sb[:], self.c_diffT.ap())
            dma(self.crows[:], self.c_crows.ap())
            dma(self.onesf_row[:], self.c_onesf.ap())
            dma(self.bq_sb[:], d["abq"])
            dma(self.bk_sb[:], d["abk"])
            dma(self.bv_sb[:], d["abv"])
            dma(self.bg_sb[:], d["abg"])
            dma(self.bmb_sb[:], d["abmb"])
            dma(self.normT_sb[:], d["anormT"])
            dma(self.bnw_sb[:], d["bnw"])
            dma(self.ld_row[0:1, :], d["bld"].unsqueeze(0))
            return self.dstr[10].n
        ldn = self.sec("Y", y)
        def g(e: Eng):
            self.dstr[11].inc(e.dma_start(out=self.normT_bf[:], in_=d["anormT"]), 16)
            self.dstr[11].inc(e.dma_start(out=self.fgk_bf[:], in_=d["fgk"]), 16)
            return self.dstr[11].n
        gldn = self.sec("G", g)
        self.sec("V", lambda e: e.memset(self.jdump[:], 0.0))
        self.const_tok = ldn
        self.gconst_tok = gldn

    def wconsts(self, e: Eng):
        e.wait(self.dstr[10], self.const_tok)
        e.wait(self.dstr[11], self.gconst_tok)

    def emit_boundary(self, src, dst, has_delta, write_xn):
        nc = self.nc
        self.uid += 1
        u = self.uid
        with contextlib.ExitStack() as ps:
            sb = lambda name, shape, dt: ps.enter_context(nc.sbuf_tensor(f"{name}_{u}", shape, dt))
            xl = sb("bn_xl", [128, 2, HALF], F32)
            xp = sb("bn_xp", [128, 2, HALF], F32)
            xcb = sb("bn_xcb", [128, 2, HALF], BF16)
            sqb = sb("bn_sqb", [128, 2, HALF], BF16)
            mB = sb("bn_mB", [128, HALF], F32)
            exq = sb("bn_exq", [128, HALF], F32)
            uu = sb("bn_uu", [128, HALF], F32)
            rB = sb("bn_rB", [128, HALF], F32)
            xl2 = sb("bn_xl2", [128, 2, HALF], F32)
            tmf = sb("bn_tmf", [128, 2, HALF], F32)
            psS = self.psBig.ap()[:, 0:2]
            psQ = self.psBig.ap()[:, 2:4]

            stats_done = 0
            prev_g15 = 0
            NT2 = 2 * NF
            x_tok = [0] * NT2
            st_tok = [None] * NT2
            se_tok = [0] * NT2
            mm_tok = [0] * NT2
            a_tok = [0] * NT2
            g_tok = [0] * NT2
            ld_tok = [None] * NT2
            ld2_tok = [None] * NT2
            for hf in range(2):
                th = slice(hf * HALF, (hf + 1) * HALF)
                for fc in range(NF):
                    gi = hf * NF + fc
                    par = fc % 2
                    rows = slice(fc * 128, (fc + 1) * 128)

                    def y1(e: Eng, fc=fc, gi=gi, par=par, rows=rows):
                        if gi >= 2:
                            if has_delta:
                                e.wait(self.vv, x_tok[gi - 2])
                            else:
                                if write_xn:
                                    e.wait(self.se, se_tok[gi - 2])
                                if dst is not None:
                                    self.swait(e, st_tok[gi - 2])
                        ld_tok[gi] = self.sdma(e, par, xl[:, par], src[rows, th])
                    self.sec("Y", y1)

                    if has_delta:
                        def v1(e: Eng, fc=fc, gi=gi, par=par):
                            self.swait(e, ld_tok[gi])
                            if gi >= 2:
                                if dst is not None:
                                    self.swait(e, st_tok[gi - 2])
                                if write_xn:
                                    e.wait(self.se, se_tok[gi - 2])
                            x_tok[gi] = self.vv.inc(
                                e.tensor_add(xp[:, par], xl[:, par], self.delta[:, fc, th]))
                        self.sec("V", v1)
                        xsrc = xp
                    else:
                        xsrc = xl

                    if dst is not None:
                        def y2(e: Eng, fc=fc, gi=gi, par=par, rows=rows, xsrc=xsrc):
                            if has_delta:
                                e.wait(self.vv, x_tok[gi])
                            else:
                                self.swait(e, ld_tok[gi])
                            st_tok[gi] = self.sdma(e, 4 + par, dst[rows, th], xsrc[:, par])
                        self.sec("Y", y2)

                    if write_xn:
                        def s1(e: Eng, fc=fc, gi=gi, par=par, xsrc=xsrc):
                            if has_delta:
                                e.wait(self.vv, x_tok[gi])
                            else:
                                self.swait(e, ld_tok[gi])
                            if gi >= 2:
                                e.wait(self.mm, mm_tok[gi - 2])
                            e.activation(out=xcb[:, par], in_=xsrc[:, par], func=AF.Copy)
                            se_tok[gi] = self.se.inc(
                                e.activation(out=sqb[:, par], in_=xsrc[:, par], func=AF.Square))
                        self.sec("S", s1)

                        def t1(e: Eng, fc=fc, gi=gi, par=par):
                            self.wconsts(e)
                            e.wait(self.se, se_tok[gi])
                            if fc == 0:
                                e.wait(self.vv, stats_done)
                            first, last = fc == 0, fc == NF - 1
                            for tt in range(2):
                                e.matmul(psS[:, tt], self.ones_bf[:], xcb[:, par, tt * 512:(tt + 1) * 512],
                                         start=first, stop=last)
                            inst = None
                            for tt in range(2):
                                inst = e.matmul(psQ[:, tt], self.ones_bf[:], sqb[:, par, tt * 512:(tt + 1) * 512],
                                                start=first, stop=last)
                            mm_tok[gi] = self.mm.inc(inst)
                        self.sec("T", t1)
                if not write_xn:
                    if dst is not None and hf == 1:
                        def ytail(e: Eng):
                            self.swait(e, st_tok[NT2 - 2])
                            self.swait(e, st_tok[NT2 - 1])
                        self.sec("Y", ytail)
                    continue

                def v2(e: Eng, hf=hf):
                    e.wait(self.mm, mm_tok[hf * NF + NF - 1])
                    e.wait(self.ge, prev_g15)
                    e.tensor_scalar(mB[:], psS.rearrange("p a b -> p (a b)"),
                                    scalar1=-1.0 / D, scalar2=None, op0=ALU.mult)
                    t1_ = self.vv.inc(e.tensor_scalar(exq[:], psQ.rearrange("p a b -> p (a b)"),
                                                      scalar1=1.0 / D, scalar2=None, op0=ALU.mult))
                    e.wait(self.vv, t1_)
                    t2_ = self.vv.inc(e.tensor_mul(uu[:], mB[:], mB[:]))
                    e.wait(self.vv, t2_)
                    return self.vv.inc(e.tensor_sub(exq[:], exq[:], uu[:]))
                var_tok = self.sec("V", v2)
                stats_done = var_tok

                def s2(e: Eng):
                    e.wait(self.vv, var_tok)
                    return self.se.inc(e.activation(out=uu[:], in_=exq[:], func=AF.Sqrt,
                                                    bias=self.crows[:, C_LNEPS:C_LNEPS + 1]))
                srt_tok = self.sec("S", s2)

                def v3(e: Eng):
                    e.wait(self.se, srt_tok)
                    return self.vv.inc(e.reciprocal(rB[:], uu[:]))
                r_tok = self.sec("V", v3)

                src2 = dst if dst is not None else src
                for fc in range(NF):
                    gi = hf * NF + fc
                    par = fc % 2
                    rows = slice(fc * 128, (fc + 1) * 128)

                    def y3(e: Eng, fc=fc, gi=gi, par=par, rows=rows):
                        if dst is not None:
                            self.swait(e, st_tok[gi])
                        if gi >= 2:
                            e.wait(self.vv, a_tok[gi - 2])
                        ld2_tok[gi] = self.sdma(e, 2 + par, xl2[:, par], src2[rows, th])
                    self.sec("Y", y3)

                    def v4(e: Eng, fc=fc, gi=gi, par=par, r_tok=r_tok):
                        self.swait(e, ld2_tok[gi])
                        e.wait(self.vv, r_tok)
                        if gi >= 2:
                            e.wait(self.ge, g_tok[gi - 2])
                        a_tok[gi] = self.vv.inc(e.tensor_add(tmf[:, par], xl2[:, par], mB[:]))
                    self.sec("V", v4)

                    def g1(e: Eng, fc=fc, gi=gi, par=par, th=th):
                        e.wait(self.vv, a_tok[gi])
                        g_tok[gi] = self.ge.inc(e.tensor_mul(self.xn[:, fc, th], tmf[:, par], rB[:]))
                    self.sec("G", g1)
                prev_g15 = g_tok[hf * NF + NF - 1]
        self.join()

    def emit_dump(self, name, which):
        if name not in self.dbg:
            return
        src = {"xn": self.xn, "delta": self.delta}[which]
        nc = self.nc
        with contextlib.ExitStack() as ps:
            self.uid += 1
            tmp = ps.enter_context(nc.sbuf_tensor(f"dbg_tmp_{self.uid}", [128, S], F32))
            for fc in range(NF):
                def vc(e: Eng, fc=fc):
                    return self.vv.inc(e.tensor_copy(tmp[:], src[:, fc]))
                t = self.sec("V", vc)
                def yd(e: Eng, fc=fc, t=t):
                    e.wait(self.vv, t)
                    return self.sdma(e, 4, self.dbg[name][:, fc], tmp[:])
                t2 = self.sec("Y", yd)
                def vw(e: Eng, t2=t2):
                    self.swait(e, t2)
                self.sec("V", vw)
        self.join()

    def emit_ffn(self, final_src=None):
        nc = self.nc
        d = self.d
        self.uid += 1
        u = self.uid
        with contextlib.ExitStack() as ps:
            sb = lambda name, shape, dt: ps.enter_context(nc.sbuf_tensor(f"{name}_{u}", shape, dt))
            if final_src is not None:
                xr = sb("ff_xr", [128, 2, HALF], F32)
                fin_prev_st = [None, None]
            hh = sb("ff_h", [128, NF, HALF], BF16)
            htmp = sb("ff_htmp", [128, 2, HALF], BF16)
            cwb = sb("ff_cwb", [128, S], BF16)
            glog = sb("ff_glog", [128, NF, NEXP], F32)
            m1 = sb("ff_m1", [128, NF], F32)
            m2 = sb("ff_m2", [128, NF], F32)
            eq1 = sb("ff_eq1", [128, NF, NEXP], F32)
            eq2 = sb("ff_eq2", [128, NF, NEXP], F32)
            gtmp = sb("ff_gtmp", [128, NF, NEXP], F32)
            dfe = sb("ff_dfe", [128, NF], F32)
            w1 = sb("ff_w1", [128, NF], F32)
            w2 = sb("ff_w2", [128, NF], F32)
            cwf = sb("ff_cwf", [128, NF, NEXP], F32)
            cw_bf = sb("ff_cwbf", [128, NF, NEXP], BF16)

            NEXP_LIM = int(os.environ.get("KF_NEXP_LIM", NEXP))
            SKIP_GATING = os.environ.get("KF_SKIP_GATING", "") == "1"
            SKIP_CWB = os.environ.get("KF_SKIP_CWB", "") == "1"
            if SKIP_GATING:
                def vq(e: Eng):
                    t = self.vv.inc(e.memset(cwf[:], 0.25))
                    e.wait(self.vv, t)
                    return self.vv.inc(e.tensor_copy(cw_bf[:], cwf[:]))
                cw_tok = self.sec("V", vq)
            psA = self.psBig.ap()[:, 0:2]
            psB = self.psBig.ap()[:, 2:4]
            pscw = self.psBig2.ap().rearrange("p a b -> p (a b)")
            psg = pscw[:, 0:128].rearrange("p (a b) -> p a b", b=NEXP)
            if True:
                if SKIP_GATING:
                    pass
                else:
                    def t0(e: Eng):
                        self.wconsts(e)
                        inst = None
                        for tc in range(NF):
                            for fc in range(NF):
                                inst = e.matmul(psg[:, tc, :], self.xn[:, fc, tc * 128:(tc + 1) * 128],
                                                self.fgk_bf[:, fc], start=(fc == 0), stop=(fc == NF - 1))
                        return self.mm.inc(inst)
                    gmm = self.sec("T", t0)

                    def vg(e: Eng):
                        e.wait(self.mm, gmm)
                        self.wconsts(e)
                        t = self.vv.inc(e.tensor_copy(glog[:], psg))
                        e.wait(self.vv, t)
                        t = self.vv.inc(e.reduce_max(m1[:], glog[:], axis=AX.X))
                        e.wait(self.vv, t)
                        t = self.vv.inc(e.tensor_tensor(
                            eq1[:], glog[:], m1[:].unsqueeze(2).broadcast_to([128, NF, NEXP]), ALU.is_equal))
                        e.wait(self.vv, t)
                        t = self.vv.inc(e.scalar_tensor_tensor(
                            out=gtmp[:], in0=eq1[:], scalar=self.crows[:, C_N1E30:C_N1E30 + 1], in1=glog[:],
                            op0=ALU.mult, op1=ALU.add))
                        e.wait(self.vv, t)
                        t = self.vv.inc(e.reduce_max(m2[:], gtmp[:], axis=AX.X))
                        e.wait(self.vv, t)
                        e.tensor_tensor(eq2[:], gtmp[:], m2[:].unsqueeze(2).broadcast_to([128, NF, NEXP]),
                                        ALU.is_equal)
                        t = self.vv.inc(e.tensor_sub(dfe[:], m2[:], m1[:]))
                        e.wait(self.vv, t)
                        return t
                    dfe_tok = self.sec("V", vg)

                    def sg(e: Eng):
                        e.wait(self.vv, dfe_tok)
                        return self.se.inc(e.activation(out=dfe[:], in_=dfe[:], func=AF.Exp))
                    e2_tok = self.sec("S", sg)

                    def vg2(e: Eng):
                        e.wait(self.se, e2_tok)
                        t = self.vv.inc(e.tensor_scalar(w1[:], dfe[:], scalar1=1.0, scalar2=None, op0=ALU.add))
                        e.wait(self.vv, t)
                        t = self.vv.inc(e.reciprocal(w1[:], w1[:]))
                        e.wait(self.vv, t)
                        t = self.vv.inc(e.tensor_mul(w2[:], dfe[:], w1[:]))
                        e.wait(self.vv, t)
                        e.tensor_tensor(gtmp[:], eq2[:], w2[:].unsqueeze(2).broadcast_to([128, NF, NEXP]), ALU.mult)
                        t = self.vv.inc(e.tensor_tensor(
                            cwf[:], eq1[:], w1[:].unsqueeze(2).broadcast_to([128, NF, NEXP]), ALU.mult))
                        e.wait(self.vv, t)
                        t = self.vv.inc(e.tensor_add(cwf[:], cwf[:], gtmp[:]))
                        e.wait(self.vv, t)
                        return self.vv.inc(e.tensor_copy(cw_bf[:], cwf[:]))
                    cw_tok = self.sec("V", vg2)

            PH = os.environ.get("KF_PHASES", "h0,h1,gh,wo").split(",")
            mats = [m for m in range(3) if ("h0", "h1", "wo")[m] in PH]
            blocks = {}
            for e_ in range(NEXP_LIM):
                for hf in range(2):
                    for mat, wname in ((0, "fwi0"), (1, "fwi1"), (2, "fwo")):
                        if mat not in mats:
                            continue
                        for mc in range(NF):
                            blocks[(e_, hf, mat, mc)] = self.wadd(
                                d[wname][e_, :, mc * 128:(mc + 1) * 128]
                                .rearrange("(fc p) m -> p fc m", p=128))
            self.sec("G", lambda g: self.wloads(g))

            nblk = [self.wc.n]
            blk0 = self.wc.n
            cwb_mm = [0]
            cwb_tok = [0]
            h1ev = {}
            hev = {}
            silu = {}
            blk_ev = {}

            for e_ in range(NEXP_LIM):
                def tcw(e: Eng, e_=e_):
                    if SKIP_CWB:
                        return
                    e.wait(self.vv, cw_tok)
                    if e_ > 0:
                        e.wait(self.vv, hev[(1, NF - 1)])
                    inst = None
                    for tc in range(NF):
                        inst = e.matmul(pscw[:, tc * 128:(tc + 1) * 128],
                                        cw_bf[:, tc, e_:e_ + 1].broadcast_to([128, 128]),
                                        self.ident_bf[:], start=True, stop=True)
                    cwb_mm[0] = self.mm.inc(inst)
                self.sec("T", tcw)

                def vcw(e: Eng):
                    if SKIP_CWB:
                        cwb_tok[0] = self.vv.inc(e.memset(cwb[:], 0.25))
                        return
                    e.wait(self.mm, cwb_mm[0])
                    cwb_tok[0] = self.vv.inc(e.tensor_copy(cwb[:], pscw[:]))
                self.sec("V", vcw)

                for hf in range(2):
                    th = slice(hf * HALF, (hf + 1) * HALF)

                    def th0(e: Eng, e_=e_, hf=hf):
                        se_base = self.se.n
                        for mc in range(NF):
                            i = nblk[0]
                            w = self.wget(e, blocks[(e_, hf, 0, mc)]).rearrange("p (fc m) -> p fc m", m=128)
                            if i - blk0 >= 2:
                                s_, t_ = blk_ev[i - 2]
                                e.wait(s_, t_)
                            ps_ = (psA, psB)[i % 2]
                            inst = None
                            for fc in range(NF):
                                for tt in range(2):
                                    inst = e.matmul(
                                        ps_[:, tt], w[:, fc],
                                        self.xn[:, fc, hf * HALF + tt * 512: hf * HALF + (tt + 1) * 512],
                                        start=(fc == 0), stop=(fc == NF - 1))
                            self.wc.inc(inst)
                            blk_ev[i] = (self.se, se_base + mc + 1)
                            nblk[0] += 1

                    def sh0(e: Eng, e_=e_, hf=hf):
                        base = nblk[0] - NF
                        for mc in range(NF):
                            i = base + mc
                            e.wait(self.wc, i + 1)
                            if (hf, mc) in hev:
                                e.wait(self.vv, hev[(hf, mc)])
                            ps_ = (psA, psB)[i % 2]
                            silu[(hf, mc)] = self.se.inc(
                                e.activation(out=hh[:, mc], in_=ps_[:].rearrange("p a b -> p (a b)"),
                                             func=AF.Silu))
                            assert silu[(hf, mc)] == blk_ev[i][1], "se token prediction drift"

                    if "h0" in PH:
                        self.sec("T", th0)
                        self.sec("S", sh0)

                    def th1_(e: Eng, e_=e_, hf=hf):
                        vv_base = self.vv.n
                        for mc in range(NF):
                            i = nblk[0]
                            w = self.wget(e, blocks[(e_, hf, 1, mc)]).rearrange("p (fc m) -> p fc m", m=128)
                            if i - blk0 >= 2:
                                s_, t_ = blk_ev[i - 2]
                                e.wait(s_, t_)
                            ps_ = (psA, psB)[i % 2]
                            inst = None
                            for fc in range(NF):
                                for tt in range(2):
                                    inst = e.matmul(
                                        ps_[:, tt], w[:, fc],
                                        self.xn[:, fc, hf * HALF + tt * 512: hf * HALF + (tt + 1) * 512],
                                        start=(fc == 0), stop=(fc == NF - 1))
                            self.wc.inc(inst)
                            blk_ev[i] = (self.vv, vv_base + 2 * mc + 1)
                            nblk[0] += 1

                    def vh1(e: Eng, e_=e_, hf=hf):
                        base = nblk[0] - NF
                        for mc in range(NF):
                            i = base + mc
                            e.wait(self.wc, i + 1)
                            e.wait(self.vv, cwb_tok[0])
                            e.wait(self.se, silu[(hf, mc)])
                            t_ = self.vv.inc(
                                e.tensor_tensor(htmp[:, mc % 2], ps_flatten((psA, psB)[i % 2]),
                                                cwb[:, th], ALU.mult))
                            assert t_ == blk_ev[i][1], "vv token prediction drift"
                            e.wait(self.vv, t_)
                            hev[(hf, mc)] = self.vv.inc(
                                e.tensor_mul(hh[:, mc], hh[:, mc], htmp[:, mc % 2]))
                            h1ev[(hf, mc)] = hev[(hf, mc)]

                    if "h1" in PH:
                        self.sec("T", th1_)
                        self.sec("V", vh1)

                    fin = final_src is not None and e_ == NEXP_LIM - 1

                    def two(e: Eng, e_=e_, hf=hf, fin=fin):
                        vv_base = self.vv.n
                        for dc in range(NF):
                            i = nblk[0]
                            w = self.wget(e, blocks[(e_, hf, 2, dc)]).rearrange("p (mc dd) -> p mc dd", dd=128)
                            if i - blk0 >= 2:
                                s_, t_ = blk_ev[i - 2]
                                e.wait(s_, t_)
                            if dc == 0:
                                e.wait(self.vv, hev[(hf, NF - 1)])
                            ps_ = (psA, psB)[i % 2]
                            inst = None
                            for mc in range(NF):
                                for tt in range(2):
                                    inst = e.matmul(ps_[:, tt], w[:, mc],
                                                    hh[:, mc, tt * 512:(tt + 1) * 512],
                                                    start=(mc == 0), stop=(mc == NF - 1))
                            self.wc.inc(inst)
                            blk_ev[i] = (self.vv, vv_base + (2 * dc + 1 if fin else dc + 1))
                            nblk[0] += 1

                    def vwo(e: Eng, e_=e_, hf=hf):
                        base = nblk[0] - NF
                        for dc in range(NF):
                            i = base + dc
                            e.wait(self.wc, i + 1)
                            ps_ = (psA, psB)[i % 2]
                            pflat = ps_[:].rearrange("p a b -> p (a b)")
                            if e_ == 0:
                                t = self.vv.inc(e.tensor_copy(self.delta[:, dc, th], pflat))
                            else:
                                t = self.vv.inc(e.tensor_add(self.delta[:, dc, th],
                                                             self.delta[:, dc, th], pflat))
                            assert t == blk_ev[i][1], "vv token prediction drift (wo)"

                    if "wo" in PH:
                        if fin:
                            pst = (fin_prev_st[0], fin_prev_st[1])
                            V0 = self.vv.n
                            ld_tok = [None] * NF
                            st_tok2 = [None] * NF

                            def yio(e: Eng, hf=hf, V0=V0):
                                for dc in range(NF):
                                    par = dc % 2
                                    if dc < 2 and pst[par] is not None:
                                        self.swait(e, pst[par])
                                    if dc >= 2:
                                        e.wait(self.vv, V0 + 2 * (dc - 1))
                                        rows2 = slice((dc - 2) * 128, (dc - 1) * 128)
                                        st_tok2[dc - 2] = self.sdma(
                                            e, 4 + par, self.out_d[rows2, th], xr[:, par])
                                        self.swait(e, st_tok2[dc - 2])
                                    rows = slice(dc * 128, (dc + 1) * 128)
                                    ld_tok[dc] = self.sdma(e, 2 + par, xr[:, par],
                                                           final_src[rows, th])
                                for dc in (NF - 2, NF - 1):
                                    par = dc % 2
                                    e.wait(self.vv, V0 + 2 * (dc + 1))
                                    rows = slice(dc * 128, (dc + 1) * 128)
                                    st_tok2[dc] = self.sdma(e, 4 + par, self.out_d[rows, th],
                                                            xr[:, par])
                            self.sec("Y", yio)
                            fin_prev_st[0] = st_tok2[NF - 2]
                            fin_prev_st[1] = st_tok2[NF - 1]
                            self.sec("T", two)

                            def vwo_f(e: Eng, hf=hf, V0=V0, pst=pst):
                                base = nblk[0] - NF
                                for dc in range(NF):
                                    i = base + dc
                                    par = dc % 2
                                    e.wait(self.wc, i + 1)
                                    self.swait(e, ld_tok[dc])
                                    if dc >= 2:
                                        self.swait(e, st_tok2[dc - 2])
                                    elif pst[par] is not None:
                                        self.swait(e, pst[par])
                                    ps_ = (psA, psB)[i % 2]
                                    pflat = ps_[:].rearrange("p a b -> p (a b)")
                                    t = self.vv.inc(e.tensor_add(xr[:, par], xr[:, par], pflat))
                                    assert t == blk_ev[i][1], "vv token prediction drift (fin)"
                                    e.wait(self.vv, t)
                                    self.vv.inc(e.tensor_add(xr[:, par], xr[:, par],
                                                             self.delta[:, dc, th]))
                            self.sec("V", vwo_f)
                        else:
                            self.sec("T", two)
                            self.sec("V", vwo)
        self.join()

    def emit_attn_a(self):
        nc = self.nc
        d = self.d
        self.uid += 1
        u = self.uid
        SH = 8

        kblk = {}
        for kv in range(NKV):
            for sh in range(2):
                kblk[(kv, sh, "k")] = self.wadd(
                    d["aWkT"][:, kv * 128:(kv + 1) * 128].rearrange("(fc p) m -> p fc m", p=128))
                kblk[(kv, sh, "v")] = self.wadd(
                    d["aWvT"][:, kv * 128:(kv + 1) * 128].rearrange("(fc p) m -> p fc m", p=128))
                for h4 in range(4):
                    h = kv * 4 + h4
                    kblk[(kv, sh, "mb", h4)] = self.wadd(
                        d["aWmbT"][:, h * 128:(h + 1) * 128].rearrange("(fc p) m -> p fc m", p=128))
        qblk = {}
        for h in range(NH):
            qblk[(h, "q")] = self.wadd(
                d["aWqT"][:, h * 128:(h + 1) * 128].rearrange("(fc p) m -> p fc m", p=128))
            qblk[(h, "g")] = self.wadd(
                d["aWgT"][:, h * 128:(h + 1) * 128].rearrange("(fc p) m -> p fc m", p=128))

        with contextlib.ExitStack() as ps:
            sb = lambda name, shape, dt: ps.enter_context(nc.sbuf_tensor(f"{name}_{u}", shape, dt))
            kfm = sb("aa_fm", [128, HALF], BF16)
            vfm = kfm
            mbfm = kfm
            ktm = sb("aa_ktm", [128, SH, 128], BF16)
            vtm = sb("aa_vtm", [128, SH, 128], BF16)
            mbtm = sb("aa_mbtm", [128, SH, 128], BF16)
            base = sb("aa_base", [128, SH, 256], BF16)
            mk_tm = sb("aa_mktm", [128, SH, FDIM], BF16)
            mk_fm = sb("aa_mkfm", [128, NFC, HALF], BF16)
            mem_sb = sb("aa_mem", [128, NFC, 512], BF16)
            num_bf = sb("aa_num", [128, SH, 512], BF16)
            Sall = sb("aa_Sall", [128, SH], F32)
            uua = sb("aa_u", [128, SH], F32)
            rra = sb("aa_r", [128, SH], F32)
            sinv = sb("aa_sinv", [128, SH], F32)
            u1e8 = sb("aa_u1e8", [128, SH], F32)
            usv = sb("aa_usv", [128, SH], F32)
            denf = sb("aa_den", [128, SH, 4], F32)
            denr = sb("aa_denr", [128, SH, 4], F32)
            coefb = sb("aa_coef", [128, SH, 4], BF16)
            wmv = sb("aa_wmv", [128, SH, 128], BF16)
            pvt = sb("aa_pv", [128, SH, 128], BF16)
            dmem_st = sb("aa_dst", [128, NFC, 128], F32)
            dn_acc = sb("aa_dna", [128, NFC, 4], F32)
            psP = self.psBig.ap()[:, 0:2]
            psN = self.psBig.ap()[:, 2]
            psDen = self.psBig.ap()[:, 3, 0:32]
            psT = self.psBig2.ap()[:, 3].bitcast(BF16)[:, 0:512]
            psDM = self.psBig2.ap().rearrange("p a b -> p (a b)")[:, 0:768].rearrange(
                "p (a b) -> p a b", b=128)
            psDN = self.psBig2.ap()[:, 2, 0:24].rearrange("p (a b) -> p a b", b=4)

            self.aa_psP_free = 0
            self.aa_psT_free = 0
            self.aa_psN_last = 0
            self.aa_psDen_free = 0
            self.aa_dm_ev = 0
            self.aa_dn_ev = 0
            self.aa_sq_last = 0
            self.aa_wmv_free = 0
            self.aa_mk_readers = 0
            self.aa_mkfm_readers = 0
            self.aa_fm_free = 0

            mem_tok = [None] * NKV
            kv_done_v = [0] * NKV
            cc_store_toks = []
            dmem_store = {}
            dn_store = {}
            dmev = {}

            def proj(which, kv, sh, blk, fmbuf, bias_ap, act):
                def tp(e: Eng):
                    w = self.wget(e, blk).rearrange("p (fc m) -> p fc m", m=128)
                    e.wait(self.se, self.aa_psP_free)
                    inst = None
                    for fc in range(NF):
                        for tt in range(2):
                            inst = e.matmul(psP[:, tt], w[:, fc],
                                            self.xn[:, fc, sh * HALF + tt * 512: sh * HALF + (tt + 1) * 512],
                                            start=(fc == 0), stop=(fc == NF - 1))
                    self.wc.inc(inst)
                    return self.wc.n
                ptok = self.sec("T", tp)

                def sev(e: Eng):
                    e.wait(self.wc, ptok)
                    e.wait(self.vv, self.aa_fm_free)
                    t = self.se.inc(e.activation(out=fmbuf[:], in_=ps_flatten(psP), func=act,
                                                 bias=bias_ap))
                    self.aa_psP_free = t
                    return t
                return self.sec("S", sev)

            for kv in range(NKV):
                def gmem(e: Eng, kv=kv):
                    if kv >= 1:
                        e.wait(self.vv, kv_done_v[kv - 1])
                    toks = []
                    for h4 in range(4):
                        sidx = 12 if h4 % 2 == 0 else 15
                        if h4 >= 2:
                            self.swait(e, toks[h4 - 2])
                        toks.append(self.sdma(
                            e, sidx, mem_sb[:, :, h4 * 128:(h4 + 1) * 128],
                            d["amem"][kv * 4 + h4].rearrange("(fcc p) dd -> p fcc dd", p=128)))
                    mem_tok[kv] = toks
                self.sec("G", gmem)

                for sh in range(2):
                    th = slice(sh * HALF, (sh + 1) * HALF)
                    self.sec("G", lambda g: self.wloads_n(g, 6))

                    ket = proj("k", kv, sh, kblk[(kv, sh, "k")], kfm, self.bk_sb[:, kv:kv + 1], AF.Identity)
                    prs = [(kfm[:, tc * 128:(tc + 1) * 128], ktm[:, tc]) for tc in range(SH)]
                    ktm_tok = self.emit_transposes_after_se(prs, psT, ket,
                                                           dst_guard=("se", self.aa_sq_last))
                    self.aa_fm_free = ktm_tok

                    vet = proj("v", kv, sh, kblk[(kv, sh, "v")], vfm, self.bv_sb[:, kv:kv + 1], AF.Identity)
                    prs = [(vfm[:, tc * 128:(tc + 1) * 128], vtm[:, tc]) for tc in range(SH)]
                    vtm_tok = self.emit_transposes_after_se(prs, psT, vet)
                    self.aa_fm_free = vtm_tok

                    def sbase(e: Eng, ktm_tok=ktm_tok):
                        e.wait(self.vv, ktm_tok)
                        e.wait(self.se, self.aa_sq_last)
                        e.wait(self.mm, self.aa_mk_readers)
                        e.activation(out=base[:, :, 0:128], in_=ktm[:], func=AF.Relu)
                        return self.se.inc(e.activation(out=base[:, :, 128:256], in_=ktm[:],
                                                        func=AF.Relu,
                                                        scale=self.crows[:, C_NEG1:C_NEG1 + 1]))
                    btok = self.sec("S", sbase)

                    def vmk(e: Eng, btok=btok):
                        e.wait(self.se, btok)
                        t = 0
                        for j in (1, 2, 3):
                            o = (j - 1) * 256
                            t = self.vv.inc(e.tensor_mul(
                                mk_tm[:, :, o + j:o + 256], base[:, :, j:256], base[:, :, 0:256 - j]))
                            t = self.vv.inc(e.tensor_mul(
                                mk_tm[:, :, o:o + j], base[:, :, 0:j], base[:, :, 256 - j:256]))
                        return t
                    mk_all = self.sec("V", vmk)
                    mk_toks = [mk_all] * SH

                    sq_toks = [0] * SH
                    for tc in range(SH):
                        def ssq(e: Eng, tc=tc, mk_all=mk_all):
                            e.wait(self.vv, mk_all)
                            if tc >= 1:
                                e.wait(self.se, sq_toks[tc - 1])
                            return self.se.inc(e.activation(out=self.dumpb[:], in_=mk_tm[:, tc],
                                                            func=AF.Square,
                                                            accum_out=Sall[:, tc:tc + 1]))
                        sq_toks[tc] = self.sec("S", ssq)
                    self.aa_sq_last = sq_toks[SH - 1]

                    def sch(e: Eng):
                        e.wait(self.se, sq_toks[SH - 1])
                        return self.se.inc(e.activation(out=uua[:], in_=Sall[:], func=AF.Sqrt))
                    ntok = self.sec("S", sch)

                    def vch(e: Eng, ntok=ntok):
                        e.wait(self.se, ntok)
                        t = self.vv.inc(e.tensor_scalar(uua[:], uua[:],
                                                        scalar1=self.crows[:, C_1E12:C_1E12 + 1],
                                                        scalar2=None, op0=ALU.max))
                        e.wait(self.vv, t)
                        t = self.vv.inc(e.reciprocal(rra[:], uua[:]))
                        t = self.vv.inc(e.reciprocal(sinv[:], Sall[:]))
                        e.wait(self.vv, t)
                        t = self.vv.inc(e.tensor_scalar(u1e8[:], uua[:],
                                                        scalar1=self.crows[:, C_1E8:C_1E8 + 1],
                                                        scalar2=None, op0=ALU.mult))
                        t = self.vv.inc(e.tensor_mul(usv[:], uua[:], sinv[:]))
                        return t
                    sch_tok = self.sec("V", vch)

                    prs = []
                    for fcc in range(NFC):
                        for tc in range(SH):
                            prs.append((mk_tm[:, tc, fcc * 128:(fcc + 1) * 128],
                                        mk_fm[:, fcc, tc * 128:(tc + 1) * 128]))
                    mkfm_tok = self.emit_transposes_after_vv(prs, psT, mk_toks[SH - 1],
                                                             dst_guard=("mm", self.aa_mkfm_readers))

                    num_pred = self.vv.n + 1
                    def tnd(e: Eng, kv=kv, mkfm_tok=mkfm_tok, num_pred=num_pred):
                        for tk in mem_tok[kv]:
                            self.swait(e, tk)
                        e.wait(self.vv, mkfm_tok)
                        e.wait(self.vv, self.aa_psDen_free)
                        for tc in range(SH):
                            if tc == 0:
                                e.wait(self.vv, self.aa_psN_last)
                            else:
                                e.wait(self.vv, num_pred + tc - 1)
                            inst = None
                            for h4 in range(4):
                                for fcc in range(NFC):
                                    e.matmul(psN[:, h4 * 128:(h4 + 1) * 128],
                                             mk_fm[:, fcc, tc * 128:(tc + 1) * 128],
                                             mem_sb[:, fcc, h4 * 128:(h4 + 1) * 128],
                                             start=(fcc == 0), stop=(fcc == NFC - 1))
                            for fcc in range(NFC):
                                inst = e.matmul(psDen[:, tc * 4:(tc + 1) * 4],
                                                mk_fm[:, fcc, tc * 128:(tc + 1) * 128],
                                                self.normT_bf[:, fcc, kv * 4:(kv + 1) * 4],
                                                start=(fcc == 0), stop=(fcc == NFC - 1))
                            self.mm.inc(inst)
                        return self.mm.n
                    nd_tok = self.sec("T", tnd)

                    def vnum(e: Eng, nd_tok=nd_tok, num_pred=num_pred):
                        base_mm = nd_tok - SH
                        t = 0
                        for tc in range(SH):
                            e.wait(self.mm, base_mm + tc + 1)
                            t = self.vv.inc(e.tensor_copy(num_bf[:, tc], psN))
                            assert t == num_pred + tc, "num evac token drift"
                        self.aa_psN_last = t
                        return t
                    self.sec("V", vnum)

                    def vden(e: Eng, nd_tok=nd_tok, sch_tok=sch_tok):
                        e.wait(self.mm, nd_tok)
                        e.wait(self.vv, sch_tok)
                        t = self.vv.inc(e.tensor_tensor(
                            denf[:], psDen.rearrange("p (a b) -> p a b", b=4),
                            u1e8[:].unsqueeze(2).broadcast_to([128, SH, 4]), ALU.add))
                        self.aa_psDen_free = t
                        e.wait(self.vv, t)
                        t = self.vv.inc(e.reciprocal(denr[:], denf[:]))
                        e.wait(self.vv, t)
                        t = self.vv.inc(e.tensor_tensor(
                            denf[:], denf[:], usv[:].unsqueeze(2).broadcast_to([128, SH, 4]), ALU.mult))
                        e.wait(self.vv, t)
                        t = self.vv.inc(e.tensor_scalar(denf[:], denf[:], scalar1=-1.0, scalar2=1.0,
                                                        op0=ALU.mult, op1=ALU.add))
                        e.wait(self.vv, t)
                        t = self.vv.inc(e.tensor_scalar(denf[:], denf[:], scalar1=0.0, scalar2=1.0,
                                                        op0=ALU.max, op1=ALU.min))
                        e.wait(self.vv, t)
                        t = self.vv.inc(e.tensor_tensor(
                            coefb[:], denf[:], rra[:].unsqueeze(2).broadcast_to([128, SH, 4]), ALU.mult))
                        return t
                    coef_tok = self.sec("V", vden)

                    for h4 in range(4):
                        h = kv * 4 + h4
                        met = proj("mb", kv, sh, kblk[(kv, sh, "mb", h4)], mbfm,
                                   self.bmb_sb[:, h:h + 1], AF.Sigmoid)
                        prs = [(mbfm[:, tc * 128:(tc + 1) * 128], mbtm[:, tc]) for tc in range(SH)]
                        mbtm_tok = self.emit_transposes_after_se(prs, psT, met)
                        self.aa_fm_free = mbtm_tok

                        def vwmv(e: Eng, h4=h4, mbtm_tok=mbtm_tok, coef_tok=coef_tok,
                                 vtm_tok=vtm_tok):
                            e.wait(self.vv, mbtm_tok)
                            e.wait(self.vv, coef_tok)
                            e.wait(self.vv, vtm_tok)
                            e.wait(self.mm, self.aa_wmv_free)
                            t = self.vv.inc(e.tensor_tensor(
                                pvt[:], num_bf[:, :, h4 * 128:(h4 + 1) * 128],
                                denr[:, :, h4:h4 + 1].broadcast_to([128, SH, 128]), ALU.mult))
                            e.wait(self.vv, t)
                            t = self.vv.inc(e.tensor_sub(pvt[:], vtm[:], pvt[:]))
                            e.wait(self.vv, t)
                            t = self.vv.inc(e.tensor_mul(pvt[:], pvt[:], mbtm[:]))
                            e.wait(self.vv, t)
                            t = self.vv.inc(e.tensor_tensor(
                                wmv[:], pvt[:],
                                rra[:].unsqueeze(2).broadcast_to([128, SH, 128]), ALU.mult))
                            e.wait(self.vv, t)
                            return t
                        wmv_tok = self.sec("V", vwmv)

                        def tdm(e: Eng, wmv_tok=wmv_tok):
                            e.wait(self.vv, wmv_tok)
                            e.wait(self.vv, self.aa_dm_ev)
                            inst = None
                            for fcc in range(NFC):
                                for tc in range(SH):
                                    inst = e.matmul(psDM[:, fcc],
                                                    mk_tm[:, tc, fcc * 128:(fcc + 1) * 128],
                                                    wmv[:, tc], start=(tc == 0), stop=(tc == SH - 1))
                            t = self.mm.inc(inst)
                            self.aa_wmv_free = t
                            return t
                        dm_tok = self.sec("T", tdm)

                        def vdm(e: Eng, kv=kv, sh=sh, h4=h4, dm_tok=dm_tok):
                            e.wait(self.mm, dm_tok)
                            if cc_store_toks:
                                self.swait(e, cc_store_toks[-1])
                            t = self.vv.inc(e.tensor_copy(dmem_st[:], psDM))
                            self.aa_dm_ev = t
                            return t
                        dmev[(kv, h4)] = self.sec("V", vdm)

                        def gdm(e: Eng, kv=kv, sh=sh, h4=h4):
                            h = kv * 4 + h4
                            e.wait(self.vv, dmev[(kv, h4)])
                            if cc_store_toks:
                                self.swait(e, cc_store_toks[-1])
                            tok = self.sdma_accum(
                                e, 13, self.cc_in[h * FDIM * HD:(h + 1) * FDIM * HD]
                                .rearrange("(fcc p dd) -> p fcc dd", p=128, dd=128),
                                dmem_st[:], accum=(sh == 1))
                            cc_store_toks.append(tok)
                        self.sec("G", gdm)

                    def tdn(e: Eng, coef_tok=coef_tok):
                        e.wait(self.vv, coef_tok)
                        e.wait(self.vv, self.aa_dn_ev)
                        inst = None
                        for fcc in range(NFC):
                            for tc in range(SH):
                                inst = e.matmul(psDN[:, fcc], mk_tm[:, tc, fcc * 128:(fcc + 1) * 128],
                                                coefb[:, tc], start=(tc == 0), stop=(tc == SH - 1))
                        return self.mm.inc(inst)
                    dn_tok = self.sec("T", tdn)
                    self.aa_mk_readers = dn_tok
                    self.aa_mkfm_readers = nd_tok

                    def vdn(e: Eng, kv=kv, sh=sh, dn_tok=dn_tok):
                        e.wait(self.mm, dn_tok)
                        if sh == 0:
                            if kv >= 1:
                                self.swait(e, dn_store.get(kv - 1))
                            t = self.vv.inc(e.tensor_copy(dn_acc[:], psDN))
                        else:
                            t = self.vv.inc(e.tensor_add(dn_acc[:], dn_acc[:], psDN))
                        self.aa_dn_ev = t
                        return t
                    dnev = self.sec("V", vdn)
                kv_done_v[kv] = dnev

                def gcc2(e: Eng, kv=kv, dnev=dnev):
                    e.wait(self.vv, dnev)
                    if cc_store_toks:
                        self.swait(e, cc_store_toks[-1])
                    tok = self.sdma(
                        e, 13, self.cc_in[NH * FDIM * HD + kv * 3072:NH * FDIM * HD + (kv + 1) * 3072]
                        .rearrange("(p j) -> p j", j=24),
                        dn_acc[:].rearrange("p a b -> p (a b)"))
                    cc_store_toks.append(tok)
                    dn_store[kv] = tok
                self.sec("G", gcc2)

                def gcck(e: Eng, kv=kv):
                    if cc_store_toks:
                        self.swait(e, cc_store_toks[-1])
                    fake = os.environ.get("KF_CC_FAKE", "") == "1"

                    def issue(a, b):
                        src_ap = self.cc_in[a:b].rearrange("(r c) -> r c", c=128).opt()
                        dst_ap = self.cc_out[a:b].rearrange("(r c) -> r c", c=128).opt()
                        if fake:
                            self.cc.inc(e.dma_start(out=dst_ap, in_=src_ap), 16)
                        else:
                            e.collective_compute(
                                "AllReduce", ALU.add, replica_groups=[list(range(B))],
                                ins=[src_ap], outs=[dst_ap],
                            ).then_inc(self.cc.h)
                            self.cc.n += 1
                    r0 = kv * 4 * FDIM * HD
                    for i in range(3):
                        issue(r0 + i * 1024 * 128, r0 + (i + 1) * 1024 * 128)
                    if kv == NKV - 1:
                        issue(NH * FDIM * HD, self.CCN)
                self.sec("G", gcck)
            cc_all = self.cc.n
            cc_unit = 16 if os.environ.get("KF_CC_FAKE", "") == "1" else 1
        self.join()

        with contextlib.ExitStack() as ps:
            sb = lambda name, shape, dt: ps.enter_context(nc.sbuf_tensor(f"{name}_{u}", shape, dt))
            qfm = sb("aq_qfm", [128, S], BF16)
            qbase = sb("aq_base", [128, 2, HALF], BF16)
            mq = sb("aq_mq", [128, NFC, HALF], BF16)
            sqq = sb("aq_sq", [128, HALF], BF16)
            nbu = sb("aq_nb", [128, HALF], F32)
            dau = sb("aq_da", [128, HALF], F32)
            memh = sb("aq_memh", [128, HALF], BF16)
            gbf = sb("aq_gbf", [128, HALF], BF16)
            t3b = sb("aq_t3", [128, HALF], BF16)
            nn_rb = sb("aq_nnrb", [128, NKV, NFC, 4], F32)
            nn_f32 = sb("aq_nnf", [128, NFC, NH], F32)
            nn_bf = sb("aq_nnb", [128, NH, NFC], BF16)
            nmta = sb("aq_nmta", [128, 2, NFC, 128], F32)
            nmtb = sb("aq_nmtb", [128, 2, NFC, 128], F32)
            nm_bf = sb("aq_nmbf", [128, 2, NFC, 128], BF16)
            psU = self.psBig.ap()[:, 0:2]
            psV = self.psBig.ap()[:, 2:4]

            def ynn(e: Eng, cc_all=cc_all):
                e.wait(self.cc, cc_all)
                return self.sdma(e, 2, nn_rb[:],
                                 self.cc_out[NH * FDIM * HD:]
                                 .rearrange("(kv p j) -> p kv j", p=128, j=24)
                                 .rearrange("p kv (fcc h4) -> p kv fcc h4", h4=4))
            nn_tok = self.sec("Y", ynn)

            def vnn(e: Eng, nn_tok=nn_tok):
                self.swait(e, nn_tok)
                self.wconsts(e)
                t = self.vv.inc(e.tensor_tensor(
                    nn_f32[:].rearrange("p fcc (kv h4) -> p fcc kv h4", h4=4),
                    self.normT_sb[:].rearrange("p fcc (kv h4) -> p fcc kv h4", h4=4),
                    nn_rb[:].rearrange("p kv fcc h4 -> p fcc kv h4"), ALU.add))
                e.wait(self.vv, t)
                return self.vv.inc(e.tensor_copy(nn_bf[:], nn_f32[:].rearrange("p fcc h -> p h fcc")))
            nnb_tok = self.sec("V", vnn)

            self.aq_psU_free = ("se", 0)
            self.aq_psV_free_se = 0
            self.aq_mq_free = 0
            self.aq_sq_free = 0
            self.aq_base_free = 0
            self.aq_gbf_free = 0
            self.aq_memh_free = 0
            nm_ready = {}

            for h in range(NH):
                par = h % 2
                self.sec("G", lambda g: self.wloads_n(g, 2))

                def gnm(e: Eng, h=h, par=par):
                    if h >= 2:
                        e.wait(self.vv, nm_ready[h - 2])
                    e.wait(self.cc, (3 * (h // 4) + 3) * cc_unit)
                    ta = self.sdma(e, 12 + par, nmta[:, par],
                                   d["amem"][h].rearrange("(fcc p) dd -> p fcc dd", p=128))
                    tb = self.sdma(e, 14 + par, nmtb[:, par],
                                   self.cc_out[h * FDIM * HD:(h + 1) * FDIM * HD]
                                   .rearrange("(fcc p dd) -> p fcc dd", p=128, dd=128))
                    return ta, tb
                nm_toks = self.sec("G", gnm)

                def vnm(e: Eng, h=h, par=par, nm_toks=nm_toks):
                    ta, tb = nm_toks
                    self.swait(e, ta)
                    self.swait(e, tb)
                    t = self.vv.inc(e.tensor_add(nmta[:, par], nmta[:, par], nmtb[:, par]))
                    e.wait(self.vv, t)
                    t = self.vv.inc(e.tensor_copy(nm_bf[:, par], nmta[:, par]))
                    nm_ready[h] = t
                    return t
                nm_rdy = self.sec("V", vnm)

                se_pred = self.se.n
                def tq(e: Eng, h=h, se_pred=se_pred):
                    w = self.wget(e, qblk[(h, "q")]).rearrange("p (fc m) -> p fc m", m=128)
                    toks = []
                    for qh in range(2):
                        if qh == 0:
                            kind, tok = self.aq_psU_free
                            e.wait(self.se if kind == "se" else self.vv, tok)
                        else:
                            e.wait(self.se, se_pred + 1)
                        inst = None
                        for fc in range(NF):
                            for tt in range(2):
                                inst = e.matmul(psU[:, tt], w[:, fc],
                                                self.xn[:, fc, qh * HALF + tt * 512: qh * HALF + (tt + 1) * 512],
                                                start=(fc == 0), stop=(fc == NF - 1))
                        if qh == 1:
                            self.wc.inc(inst)
                            toks.append(("wc", self.wc.n))
                        else:
                            toks.append(("mm", self.mm.inc(inst)))
                    return toks
                q_toks = self.sec("T", tq)

                def sqev(e: Eng, h=h, q_toks=q_toks, se_pred=se_pred):
                    outs = []
                    for qh in range(2):
                        kind, tok = q_toks[qh]
                        e.wait(self.wc if kind == "wc" else self.mm, tok)
                        t = self.se.inc(e.activation(
                            out=qfm[:, qh * HALF:(qh + 1) * HALF], in_=ps_flatten(psU),
                            func=AF.Identity, bias=self.bq_sb[:, h:h + 1]))
                        assert t == se_pred + qh + 1, "q evac token drift"
                        outs.append(t)
                    self.aq_psU_free = ("se", outs[-1])
                    return outs
                qev = self.sec("S", sqev)

                for qh in range(2):
                    th = slice(qh * HALF, (qh + 1) * HALF)

                    def sbase2(e: Eng, qh=qh, qev=qev):
                        e.wait(self.se, qev[qh])
                        e.wait(self.vv, self.aq_base_free)
                        e.activation(out=qbase[:, 0], in_=qfm[:, th], func=AF.Relu)
                        return self.se.inc(e.activation(out=qbase[:, 1], in_=qfm[:, th],
                                                        func=AF.Relu,
                                                        scale=self.crows[:, C_NEG1:C_NEG1 + 1]))
                    qb_tok = self.sec("S", sbase2)

                    roll_pred = self.vv.n + 1
                    def troll(e: Eng, qb_tok=qb_tok, roll_pred=roll_pred):
                        e.wait(self.se, qb_tok)
                        e.wait(self.se, self.aq_psV_free_se)
                        e.wait(self.mm, self.aq_mq_free)
                        toks = []
                        for j in (1, 2, 3):
                            for cb in range(2):
                                jc = (j - 1) * 2 + cb
                                if jc >= 1:
                                    e.wait(self.vv, roll_pred + jc - 1)
                                inst = None
                                for tt in range(2):
                                    e.matmul(psV[:, tt], self.shift_bf[:, (j - 1) * 2],
                                             qbase[:, cb, tt * 512:(tt + 1) * 512], start=True, stop=False)
                                    inst = e.matmul(psV[:, tt], self.shift_bf[:, (j - 1) * 2 + 1],
                                                    qbase[:, 1 - cb, tt * 512:(tt + 1) * 512],
                                                    start=False, stop=True)
                                toks.append(self.mm.inc(inst))
                        return toks
                    rtoks = self.sec("T", troll)

                    def vroll(e: Eng, rtoks=rtoks, roll_pred=roll_pred):
                        t = 0
                        for j in (1, 2, 3):
                            for cb in range(2):
                                jc = (j - 1) * 2 + cb
                                e.wait(self.mm, rtoks[jc])
                                t = self.vv.inc(e.tensor_mul(mq[:, jc], ps_flatten(psV), qbase[:, cb]))
                                assert t == roll_pred + jc, "roll evac token drift"
                        return t
                    roll_last = self.sec("V", vroll)
                    self.aq_base_free = roll_last

                    sq_se = [0] * NFC
                    sqmm_pred = self.mm.n + 1
                    def ssq2(e: Eng, roll_pred=roll_pred, sqmm_pred=sqmm_pred, qev=qev):
                        for jc in range(NFC):
                            e.wait(self.vv, roll_pred + jc)
                            if jc == 0:
                                e.wait(self.mm, self.aq_sq_free)
                                e.wait(self.se, qev[1])
                            else:
                                e.wait(self.mm, sqmm_pred + jc - 1)
                            sq_se[jc] = self.se.inc(e.activation(out=sqq[:], in_=mq[:, jc],
                                                                 func=AF.Square))
                    self.sec("S", ssq2)

                    def tsq(e: Eng, sqmm_pred=sqmm_pred):
                        e.wait(self.vv, self.aq_memh_free)
                        toks = []
                        for jc in range(NFC):
                            e.wait(self.se, sq_se[jc])
                            inst = None
                            for tt in range(2):
                                inst = e.matmul(psU[:, tt], self.ones_bf[:], sqq[:, tt * 512:(tt + 1) * 512],
                                                start=(jc == 0), stop=(jc == NFC - 1))
                            t = self.mm.inc(inst)
                            assert t == sqmm_pred + jc, "sq mm token drift"
                            toks.append(t)
                        self.aq_sq_free = toks[-1]
                        return toks
                    sqmm = self.sec("T", tsq)

                    def tda(e: Eng, h=h, roll_last=roll_last, nnb_tok=nnb_tok):
                        e.wait(self.vv, roll_last)
                        e.wait(self.vv, nnb_tok)
                        inst = None
                        for fcc in range(NFC):
                            for tt in range(2):
                                inst = e.matmul(psV[:, tt],
                                                nn_bf[:, h, fcc:fcc + 1].broadcast_to([128, 128]),
                                                mq[:, fcc, tt * 512:(tt + 1) * 512],
                                                start=(fcc == 0), stop=(fcc == NFC - 1))
                        return self.mm.inc(inst)
                    da_mm = self.sec("T", tda)

                    def snb(e: Eng, sqmm=sqmm):
                        e.wait(self.mm, sqmm[-1])
                        return self.se.inc(e.activation(out=nbu[:], in_=ps_flatten(psU), func=AF.Sqrt))
                    nb_tok = self.sec("S", snb)

                    def vda(e: Eng, da_mm=da_mm, nb_tok=nb_tok):
                        e.wait(self.se, nb_tok)
                        t = self.vv.inc(e.tensor_scalar(nbu[:], nbu[:],
                                                        scalar1=self.crows[:, C_1E12:C_1E12 + 1],
                                                        scalar2=None, op0=ALU.max))
                        e.wait(self.vv, t)
                        e.wait(self.mm, da_mm)
                        t = self.vv.inc(e.scalar_tensor_tensor(
                            out=dau[:], in0=nbu[:], scalar=self.crows[:, C_1E8:C_1E8 + 1],
                            in1=ps_flatten(psV), op0=ALU.mult, op1=ALU.add))
                        e.wait(self.vv, t)
                        t = self.vv.inc(e.reciprocal(dau[:], dau[:]))
                        return t
                    rr_tok = self.sec("V", vda)

                    def tna(e: Eng, h=h, nb_tok=nb_tok, nm_rdy=nm_rdy):
                        e.wait(self.se, nb_tok)
                        e.wait(self.vv, nm_rdy)
                        inst = None
                        for fcc in range(NFC):
                            for tt in range(2):
                                inst = e.matmul(psU[:, tt], nm_bf[:, h % 2, fcc],
                                                mq[:, fcc, tt * 512:(tt + 1) * 512],
                                                start=(fcc == 0), stop=(fcc == NFC - 1))
                        t = self.mm.inc(inst)
                        self.aq_mq_free = t
                        return t
                    na_mm = self.sec("T", tna)

                    def vmemh(e: Eng, na_mm=na_mm, rr_tok=rr_tok):
                        e.wait(self.mm, na_mm)
                        e.wait(self.vv, rr_tok)
                        t = self.vv.inc(e.tensor_mul(memh[:], ps_flatten(psU), dau[:]))
                        self.aq_memh_free = t
                        self.aq_psU_free = ("vv", t)
                        return t
                    memh_tok = self.sec("V", vmemh)

                    def tg(e: Eng, h=h, qh=qh, rr_tok=rr_tok):
                        w = self.wget(e, qblk[(h, "g")]).rearrange("p (fc m) -> p fc m", m=128)
                        e.wait(self.vv, rr_tok)
                        inst = None
                        for fc in range(NF):
                            for tt in range(2):
                                inst = e.matmul(psV[:, tt], w[:, fc],
                                                self.xn[:, fc, qh * HALF + tt * 512: qh * HALF + (tt + 1) * 512],
                                                start=(fc == 0), stop=(fc == NF - 1))
                        if qh == 1:
                            self.wc.inc(inst)
                            return ("wc", self.wc.n)
                        return ("mm", self.mm.inc(inst))
                    g_tok = self.sec("T", tg)

                    def sg2(e: Eng, h=h, g_tok=g_tok):
                        kind, tok = g_tok
                        e.wait(self.wc if kind == "wc" else self.mm, tok)
                        e.wait(self.vv, self.aq_gbf_free)
                        t = self.se.inc(e.activation(out=gbf[:], in_=ps_flatten(psV),
                                                     func=AF.Sigmoid, bias=self.bg_sb[:, h:h + 1]))
                        self.aq_psV_free_se = t
                        return t
                    gb_tok = self.sec("S", sg2)

                    def vfin(e: Eng, h=h, th=th, gb_tok=gb_tok, memh_tok=memh_tok):
                        e.wait(self.se, gb_tok)
                        e.wait(self.vv, memh_tok)
                        t = self.vv.inc(e.tensor_mul(t3b[:], gbf[:], memh[:]))
                        self.aq_gbf_free = t
                        e.wait(self.vv, t)
                        return self.vv.inc(e.tensor_add(self.delta[:, h, th], t3b[:], self.xn[:, h, th]))
                    self.sec("V", vfin)
        self.join()

    def emit_transposes_after_se(self, pairs, psT, se_tok, dst_guard=None):
        return self._emit_transposes(pairs, psT, ("se", se_tok), dst_guard)

    def emit_transposes_after_vv(self, pairs, psT, vv_tok, dst_guard=None):
        return self._emit_transposes(pairs, psT, ("vv", vv_tok), dst_guard)

    def _emit_transposes(self, pairs, psT, gate, dst_guard=None):
        groups = [pairs[i:i + 4] for i in range(0, len(pairs), 4)]
        pred = self.vv.n

        def tsec(e: Eng):
            toks = []
            kind, gtok = gate
            e.wait(self.se if kind == "se" else self.vv, gtok)
            for g, grp in enumerate(groups):
                if g == 0:
                    e.wait(self.vv, self.aa_psT_free)
                else:
                    e.wait(self.vv, pred + g)
                inst = None
                for k, (s_ap, _) in enumerate(grp):
                    inst = e.transpose(psT[:, k * 128:(k + 1) * 128], s_ap, self.ident_bf[:])
                toks.append(self.mm.inc(inst))
            return toks
        toks = self.sec("T", tsec)

        def vsec(e: Eng):
            t = 0
            if dst_guard is not None:
                kind, tok = dst_guard
                e.wait({"se": self.se, "mm": self.mm, "vv": self.vv}[kind], tok)
            for g, grp in enumerate(groups):
                e.wait(self.mm, toks[g])
                inst = None
                for k, (_, d_ap) in enumerate(grp):
                    inst = e.tensor_copy(d_ap, psT[:, k * 128:(k + 1) * 128])
                t = self.vv.inc(inst)
                assert t == pred + g + 1, "transpose evac token drift"
            self.aa_psT_free = t
            return t
        return self.sec("V", vsec)

    def emit_attn_b(self):
        nc = self.nc
        d = self.d
        self.uid += 1
        u = self.uid

        g1blk = self.wadd(d["bWg1"].rearrange("(fc p) m -> p fc m", p=128))
        hblk = {}
        for h in range(NH):
            for nm, wn in (("q", "bWq"), ("k", "bWk"), ("v", "bWv")):
                hblk[(h, nm)] = self.wadd(
                    d[wn][:, h * 128:(h + 1) * 128].rearrange("(fc p) m -> p fc m", p=128))
        woblk = {}
        for dc in range(NF):
            woblk[dc] = self.wadd(
                d["bWo"][:, dc * 128:(dc + 1) * 128].rearrange("(fc p) m -> p fc m", p=128))

        with contextlib.ExitStack() as ps:
            sb = lambda name, shape, dt: ps.enter_context(nc.sbuf_tensor(f"{name}_{u}", shape, dt))
            g1fm = sb("ab_g1", [128, S], BF16)
            wg2 = sb("ab_wg2", [128, S], BF16)
            qfm = sb("ab_qfm", [128, S], BF16)
            kfm = sb("ab_kfm", [128, S], BF16)
            fmst = sb("ab_fm", [128, S], BF16)
            ktm = sb("ab_ktm", [128, NTC, 128], BF16)
            vtm = sb("ab_vtm", [128, NTC, 128], BF16)
            Ssb = sb("ab_S", [128, 128], F32)
            Sbf = sb("ab_Sbf", [128, 128], BF16)
            DT = sb("ab_DT", [128, 128], F32)
            scT = sb("ab_scT", [128, 2, 128], BF16)
            kkd = sb("ab_kkd", [128, 2, 128], BF16)
            octm = sb("ab_oc", [128, 128], F32)
            o_tm = sb("ab_otm", [128, NTC, 128], BF16)
            sg = sb("ab_sg", [128, S], BF16)
            og = sb("ab_og", [128, S], BF16)
            sqf = sb("ab_sqf", [128, S], BF16)
            ssr = sb("ab_ssr", [128, NTC], F32)
            rf = sb("ab_rf", [128, NTC], F32)
            onw = sb("ab_onw", [128, S], BF16)
            qd = sb("ab_qd", [128, NH], F32)
            kd = sb("ab_kd", [128, NH], F32)
            cd = sb("ab_cd", [128, NH], F32)
            psP = self.psBig.ap()
            psT = self.psBig2.ap()[:, 0].bitcast(BF16)[:, 0:512]
            psS2 = self.psBig2.ap()[:, 1, 0:128]
            psO = self.psBig2.ap()[:, 2, 0:256]
            psU2 = self.psBig2.ap()[:, 3, 0:128]

            self.ab_psT_free = 0
            self.aa_psT_free = 0

            def gw2(e: Eng):
                return self.sdma(e, 12, wg2[:], d["bWg2"])
            wg2_tok = self.sec("G", gw2)

            def tld(e: Eng):
                self.wconsts(e)
                return self.mm.inc(e.matmul(psS2[:, 0:NH], self.onesf_row[:], self.ld_row[:],
                                            start=True, stop=True))
            ld_mm = self.sec("T", tld)

            def vld(e: Eng, ld_mm=ld_mm):
                e.wait(self.mm, ld_mm)
                return self.vv.inc(e.tensor_copy(self.ldb[:], psS2[:, 0:NH]))
            ldb_tok = self.sec("V", vld)

            def sdec(e: Eng, ldb_tok=ldb_tok):
                e.wait(self.vv, ldb_tok)
                t = 0
                for h in range(NH):
                    e.activation(out=qd[:, h:h + 1], in_=self.crows[:, C_IP1:C_IP1 + 1],
                                 func=AF.Exp, scale=self.ldb[:, h:h + 1])
                    e.activation(out=kd[:, h:h + 1], in_=self.crows[:, C_127M:C_127M + 1],
                                 func=AF.Exp, scale=self.ldb[:, h:h + 1])
                    t = self.se.inc(e.activation(out=cd[:, h:h + 1], in_=self.crows[:, C_128:C_128 + 1],
                                                 func=AF.Exp, scale=self.ldb[:, h:h + 1]))
                return t
            dec_tok = self.sec("S", sdec)

            self.sec("G", lambda g: self.wloads_n(g, 1))
            def tg1(e: Eng):
                w = self.wget(e, g1blk).rearrange("p (fc m) -> p fc m", m=128)
                inst = None
                for fc in range(NF):
                    for tt in range(4):
                        inst = e.matmul(psP[:, tt], w[:, fc],
                                        self.xn[:, fc, tt * 512:(tt + 1) * 512],
                                        start=(fc == 0), stop=(fc == NF - 1))
                self.wc.inc(inst)
                return self.wc.n
            g1_mm = self.sec("T", tg1)

            def sg1(e: Eng, g1_mm=g1_mm):
                e.wait(self.wc, g1_mm)
                return self.se.inc(e.activation(out=g1fm[:], in_=ps_flatten(psP), func=AF.Copy))
            g1_tok = self.sec("S", sg1)
            self.ab_psP_free = ("se", g1_tok)

            onw_store = {}
            pend_tr = [None]
            self.ab_fm_free = 0
            self.ab_qfm_free = 0
            self.ab_ktm_rd = 0
            self.ab_kfm_rd = 0
            self.ab_og_rd = 0
            self.ab_sg_free = 0
            self.ab_dt_free = 0
            self.ab_sxx_rd = 0

            for h in range(NH):
                self.sec("G", lambda g: self.wloads_n(g, 3))

                def tq(e: Eng, h=h):
                    w = self.wget(e, hblk[(h, "q")]).rearrange("p (fc m) -> p fc m", m=128)
                    kind, tok = self.ab_psP_free
                    e.wait(self.se if kind == "se" else self.vv, tok)
                    inst = None
                    for fc in range(NF):
                        for tt in range(4):
                            inst = e.matmul(psP[:, tt], w[:, fc],
                                            self.xn[:, fc, tt * 512:(tt + 1) * 512],
                                            start=(fc == 0), stop=(fc == NF - 1))
                    self.wc.inc(inst)
                    return self.wc.n
                q_mm = self.sec("T", tq)

                def sq(e: Eng, q_mm=q_mm):
                    e.wait(self.wc, q_mm)
                    e.wait(self.mm, self.ab_qfm_free)
                    t = self.se.inc(e.activation(out=qfm[:], in_=ps_flatten(psP), func=AF.Silu))
                    self.ab_psP_free = ("se", t)
                    return t
                q_tok = self.sec("S", sq)

                tm_toks = {}
                kfm_tok = [0]
                for nm, tmbuf, fbuf in (("k", ktm, kfm), ("v", vtm, fmst)):
                    def tkv(e: Eng, h=h, nm=nm):
                        w = self.wget(e, hblk[(h, nm)]).rearrange("p (fc m) -> p fc m", m=128)
                        kind, tok = self.ab_psP_free
                        e.wait(self.se if kind == "se" else self.vv, tok)
                        inst = None
                        for fc in range(NF):
                            for tt in range(4):
                                inst = e.matmul(psP[:, tt], w[:, fc],
                                                self.xn[:, fc, tt * 512:(tt + 1) * 512],
                                                start=(fc == 0), stop=(fc == NF - 1))
                        self.wc.inc(inst)
                        return self.wc.n
                    kv_mm = self.sec("T", tkv)

                    def skv(e: Eng, kv_mm=kv_mm, nm=nm, fbuf=fbuf):
                        e.wait(self.wc, kv_mm)
                        if nm == "k":
                            e.wait(self.mm, self.ab_kfm_rd)
                        else:
                            e.wait(self.vv, self.ab_fm_free)
                        t = self.se.inc(e.activation(out=fbuf[:], in_=ps_flatten(psP), func=AF.Silu))
                        self.ab_psP_free = ("se", t)
                        if nm == "k":
                            kfm_tok[0] = t
                        return t
                    kv_tok = self.sec("S", skv)

                    prs = [(fbuf[:, tc * 128:(tc + 1) * 128], tmbuf[:, tc]) for tc in range(NTC)]
                    tm_toks[nm] = self.emit_transposes_after_se(prs, psT, kv_tok,
                                                               dst_guard=("mm", self.ab_ktm_rd))
                    if nm == "v":
                        self.ab_fm_free = tm_toks[nm]

                if pend_tr[0] is not None:
                    pend_tr[0]()
                    pend_tr[0] = None

                def sdt(e: Eng, h=h, dec_tok=dec_tok):
                    e.wait(self.se, dec_tok)
                    e.wait(self.vv, self.ab_dt_free)
                    return self.se.inc(e.activation(out=DT[:], in_=self.diffT_sb[:],
                                                    func=AF.Exp, scale=self.ldb[:, h:h + 1]))
                dt_tok = self.sec("S", sdt)

                def vz(e: Eng):
                    e.wait(self.mm, self.ab_sxx_rd)
                    t = self.vv.inc(e.memset(Ssb[:], 0.0))
                    t = self.vv.inc(e.memset(Sbf[:], 0.0))
                    return t
                s0_tok = self.sec("V", vz)

                sc_ev = [0] * NTC
                kk_ev = [0] * NTC
                oc_ev = [0] * NTC
                ot_ev = [0] * NTC
                sbf_ev = [0] * NTC
                sc_mm = [0] * NTC
                oi_mm = [0] * NTC
                oe_mm = [0] * NTC
                su_mm = [0] * NTC
                for c in range(NTC):
                    cs = slice(c * 128, (c + 1) * 128)

                    def tsc(e: Eng, c=c, cs=cs, q_tok=q_tok, ktok=kfm_tok[0]):
                        e.wait(self.se, q_tok)
                        e.wait(self.se, ktok)
                        if c >= 1:
                            e.wait(self.vv, sc_ev[c - 1])
                        sc_mm[c] = self.mm.inc(e.matmul(psS2, kfm[:, cs], qfm[:, cs],
                                                        start=True, stop=True))
                    self.sec("T", tsc)

                    def vsc(e: Eng, c=c, dt_tok=dt_tok):
                        e.wait(self.mm, sc_mm[c])
                        e.wait(self.se, dt_tok)
                        if c >= 2:
                            e.wait(self.mm, oi_mm[c - 2])
                        sc_ev[c] = self.vv.inc(e.tensor_mul(scT[:, c % 2], psS2, DT[:]))
                    self.sec("V", vsc)

                    def vkk(e: Eng, c=c, h=h, ktok=tm_toks["k"], dec_tok=dec_tok):
                        e.wait(self.vv, ktok)
                        e.wait(self.se, dec_tok)
                        if c >= 2:
                            e.wait(self.mm, su_mm[c - 2])
                        kk_ev[c] = self.vv.inc(e.tensor_scalar(
                            kkd[:, c % 2], ktm[:, c], scalar1=kd[:, h:h + 1], scalar2=None, op0=ALU.mult))
                    self.sec("V", vkk)

                    def toi(e: Eng, c=c, cs=cs, vtok=tm_toks["v"], q_tok=q_tok):
                        e.wait(self.vv, sc_ev[c])
                        e.wait(self.vv, vtok)
                        if c >= 1:
                            e.wait(self.vv, oc_ev[c - 1])
                            e.wait(self.vv, ot_ev[c - 1])
                        oi_mm[c] = self.mm.inc(e.matmul(psO[:, 0:128], scT[:, c % 2], vtm[:, c],
                                                        start=True, stop=True))
                        e.wait(self.vv, sbf_ev[c - 1] if c >= 1 else s0_tok)
                        oe_mm[c] = self.mm.inc(e.matmul(psO[:, 128:256], qfm[:, cs], Sbf[:],
                                                        start=True, stop=True))
                        e.wait(self.vv, kk_ev[c])
                        if c >= 1:
                            e.wait(self.vv, sbf_ev[c - 1])
                        su_mm[c] = self.mm.inc(e.matmul(psU2, kkd[:, c % 2], vtm[:, c],
                                                        start=True, stop=True))
                    self.sec("T", toi)

                    def vo(e: Eng, c=c, h=h):
                        e.wait(self.mm, oi_mm[c])
                        e.wait(self.mm, oe_mm[c])
                        e.wait(self.mm, su_mm[c])
                        oc_ev[c] = self.vv.inc(e.tensor_copy(octm[:], psO[:, 0:128]))
                        e.wait(self.mm, oe_mm[c])
                        e.wait(self.vv, oc_ev[c])
                        ot_ev[c] = self.vv.inc(e.scalar_tensor_tensor(
                            out=o_tm[:, c], in0=psO[:, 128:256], scalar=qd[:, h:h + 1],
                            in1=octm[:], op0=ALU.mult, op1=ALU.add))
                        e.wait(self.mm, su_mm[c])
                        t = self.vv.inc(e.scalar_tensor_tensor(
                            out=Ssb[:], in0=Ssb[:], scalar=cd[:, h:h + 1], in1=psU2,
                            op0=ALU.mult, op1=ALU.add))
                        e.wait(self.vv, t)
                        sbf_ev[c] = self.vv.inc(e.tensor_copy(Sbf[:], Ssb[:]))
                    self.sec("V", vo)
                self.ab_sxx_rd = max(oe_mm[NTC - 1], su_mm[NTC - 1])
                self.ab_ktm_rd = su_mm[NTC - 1]
                self.ab_kfm_rd = sc_mm[NTC - 1]
                self.ab_qfm_free = oe_mm[NTC - 1]
                self.ab_dt_free = sc_ev[NTC - 1]

                def tg2(e: Eng, h=h, g1_tok=g1_tok, wg2_tok=wg2_tok):
                    e.wait(self.se, g1_tok)
                    self.swait(e, wg2_tok)
                    kind, tok = self.ab_psP_free
                    e.wait(self.se if kind == "se" else self.vv, tok)
                    inst = None
                    for tc in range(NTC):
                        inst = e.matmul(psP.rearrange("p a b -> p (a b)")[:, tc * 128:(tc + 1) * 128],
                                        g1fm[:, tc * 128:(tc + 1) * 128],
                                        wg2[:, h * 128:(h + 1) * 128], start=True, stop=True)
                    return self.mm.inc(inst)
                g2_mm = self.sec("T", tg2)

                def ssg(e: Eng, g2_mm=g2_mm):
                    e.wait(self.mm, g2_mm)
                    e.wait(self.vv, self.ab_sg_free)
                    t = self.se.inc(e.activation(out=sg[:], in_=ps_flatten(psP), func=AF.Sigmoid))
                    self.ab_psP_free = ("se", t)
                    return t
                sg_tok = self.sec("S", ssg)

                def vog(e: Eng, h=h, sg_tok=sg_tok):
                    e.wait(self.se, sg_tok)
                    e.wait(self.vv, ot_ev[NTC - 1])
                    e.wait(self.se, self.ab_og_rd)
                    t = self.vv.inc(e.tensor_mul(og[:], o_tm[:].rearrange("p a b -> p (a b)"), sg[:]))
                    self.ab_sg_free = t
                    return t
                og_tok = self.sec("V", vog)

                def ssq3(e: Eng, og_tok=og_tok):
                    e.wait(self.vv, og_tok)
                    t = self.se.inc(e.activation(out=sqf[:], in_=og[:], func=AF.Square))
                    self.ab_og_rd = t
                    return t
                sq3_tok = self.sec("S", ssq3)

                def vssr(e: Eng, sq3_tok=sq3_tok):
                    e.wait(self.se, sq3_tok)
                    t = self.vv.inc(e.tensor_reduce(ssr[:], sqf[:].rearrange("p (a b) -> p a b", b=128),
                                                    axis=AX.X, op=ALU.add))
                    e.wait(self.vv, t)
                    t = self.vv.inc(e.tensor_scalar(ssr[:], ssr[:], scalar1=1.0 / HD, scalar2=None,
                                                    op0=ALU.mult))
                    return t
                ssr_tok = self.sec("V", vssr)

                def ssrt(e: Eng, ssr_tok=ssr_tok):
                    e.wait(self.vv, ssr_tok)
                    return self.se.inc(e.activation(out=ssr[:], in_=ssr[:], func=AF.Sqrt,
                                                    bias=self.crows[:, C_LNEPS:C_LNEPS + 1]))
                srt_tok = self.sec("S", ssrt)

                def vrf(e: Eng, srt_tok=srt_tok):
                    e.wait(self.se, srt_tok)
                    t = self.vv.inc(e.reciprocal(rf[:], ssr[:]))
                    e.wait(self.vv, t)
                    last = 0
                    for tc in range(NTC):
                        last = self.vv.inc(e.tensor_scalar(
                            og[:, tc * 128:(tc + 1) * 128], og[:, tc * 128:(tc + 1) * 128],
                            scalar1=rf[:, tc:tc + 1], scalar2=None, op0=ALU.mult))
                    return last
                on_tok = self.sec("V", vrf)

                groups = [[(og[:, (g * 4 + k) * 128:(g * 4 + k + 1) * 128],
                            onw[:, (g * 4 + k) * 128:(g * 4 + k + 1) * 128]) for k in range(4)]
                          for g in range(4)]

                def fire_tr(h=h, on_tok=on_tok, groups=groups):
                    tr_pred = self.vv.n

                    def ttro(e: Eng):
                        toks = []
                        for g, grp in enumerate(groups):
                            e.wait(self.vv, on_tok)
                            if g == 0:
                                e.wait(self.vv, self.aa_psT_free)
                            else:
                                e.wait(self.vv, tr_pred + g)
                            inst = None
                            for k2, (s_ap, _) in enumerate(grp):
                                inst = e.transpose(psT[:, k2 * 128:(k2 + 1) * 128], s_ap,
                                                   self.ident_bf[:])
                            toks.append(self.mm.inc(inst))
                        return toks
                    trtoks = self.sec("T", ttro)

                    def vtro(e: Eng, trtoks=trtoks):
                        t = 0
                        if h >= 1:
                            self.swait(e, onw_store[h - 1])
                        for g, grp in enumerate(groups):
                            e.wait(self.mm, trtoks[g])
                            t = self.vv.inc(e.tensor_scalar(
                                onw[:, g * 512:(g + 1) * 512], psT[:],
                                scalar1=self.bnw_sb[:, h:h + 1], scalar2=None, op0=ALU.mult))
                            assert t == tr_pred + g + 1, "onw transpose token drift"
                        self.aa_psT_free = t
                        return t
                    onw_tok = self.sec("V", vtro)

                    def yst(e: Eng, onw_tok=onw_tok):
                        e.wait(self.vv, onw_tok)
                        if h >= 1:
                            self.swait(e, onw_store[h - 1])
                        onw_store[h] = self.sdma(e, 4, self.onw_scr[h * 128:(h + 1) * 128, :],
                                                 onw[:])
                    self.sec("Y", yst)
                pend_tr[0] = fire_tr

            if pend_tr[0] is not None:
                pend_tr[0]()
                pend_tr[0] = None

            rb_tok = [None] * NF
            last_xn_rd = self.wc.n
            for ec in range(NF):
                def yrb(e: Eng, ec=ec):
                    e.wait(self.wc, last_xn_rd)
                    self.swait(e, onw_store[NH - 1])
                    if ec >= 2:
                        self.swait(e, rb_tok[ec - 2])
                    rb_tok[ec] = self.sdma(e, ec % 2, self.xn[:, ec],
                                           self.onw_scr[ec * 128:(ec + 1) * 128, :])
                self.sec("Y", yrb)

            self.sec("G", lambda g: self.wloads_n(g, NF))
            wo_pred_v = self.vv.n
            for dc in range(NF):
                def two2(e: Eng, dc=dc):
                    w = self.wget(e, woblk[dc]).rearrange("p (fc m) -> p fc m", m=128)
                    if dc == 0:
                        kind, tok = self.ab_psP_free
                        e.wait(self.se if kind == "se" else self.vv, tok)
                    else:
                        e.wait(self.vv, wo_pred_v + dc)
                    inst = None
                    for ec in range(NF):
                        self.swait(e, rb_tok[ec])
                        for tt in range(4):
                            inst = e.matmul(psP[:, tt], w[:, ec],
                                            self.xn[:, ec, tt * 512:(tt + 1) * 512],
                                            start=(ec == 0), stop=(ec == NF - 1))
                    self.wc.inc(inst)
                    return self.wc.n
                wo_mm = self.sec("T", two2)

                def vwo2(e: Eng, dc=dc, wo_mm=wo_mm):
                    e.wait(self.wc, wo_mm)
                    t = self.vv.inc(e.tensor_copy(self.delta[:, dc], ps_flatten(psP)))
                    assert t == wo_pred_v + dc + 1, "wo evac token drift"
                    return t
                self.sec("V", vwo2)
        self.join()

    def finish(self):
        def yf(e: Eng):
            for s in self.dstr:
                e.wait(s, s.n)
        self.sec("Y", yf)
        self.stk.close()
        return self.nc


def build_program(parts=("attn_a", "ffn1", "attn_b", "ffn2"), debug_out=()):
    k = K(debug_out=debug_out)
    k.emit_consts()
    srcs = k.d["x0"]
    if not parts:
        k.emit_boundary(srcs, k.out_d, has_delta=False, write_xn=True)
        k.emit_dump("dbg_xn0", "xn")
        return k.finish()
    k.emit_boundary(srcs, None, has_delta=False, write_xn=True)
    k.emit_dump("dbg_xn0", "xn")
    for i, p in enumerate(parts):
        last = i == len(parts) - 1
        if p == "attn_a":
            k.emit_attn_a()
        elif p == "attn_b":
            k.emit_attn_b()
        elif p.startswith("ffn"):
            k.emit_ffn(final_src=srcs if last else None)
        else:
            raise ValueError(p)
        k.emit_dump(f"dbg_delta{i}", "delta")
        if last:
            if not p.startswith("ffn"):
                k.emit_boundary(srcs, k.out_d, has_delta=True, write_xn=False)
        else:
            k.emit_boundary(srcs, k.x_scr, has_delta=True, write_xn=True)
            srcs = k.x_scr
    return k.finish()


def _prep_inputs(inputs):
    a = lambda x: np.ascontiguousarray(np.asarray(x, dtype=np.float32))
    hs = a(inputs["L0_attn_a_hidden_states"])
    chunkvec = lambda v, n: np.ascontiguousarray(a(v).reshape(n, 128).T)
    common = {
        "aWqT": a(np.asarray(inputs["L0_attn_a_Wq"]).T), "aWkT": a(np.asarray(inputs["L0_attn_a_Wk"]).T),
        "aWvT": a(np.asarray(inputs["L0_attn_a_Wv"]).T), "aWgT": a(np.asarray(inputs["L0_attn_a_Wg"]).T),
        "aWmbT": a(np.asarray(inputs["L0_attn_a_Wmb"]).T),
        "abq": chunkvec(inputs["L0_attn_a_bq"], NF), "abk": chunkvec(inputs["L0_attn_a_bk"], NKV),
        "abv": chunkvec(inputs["L0_attn_a_bv"], NKV), "abg": chunkvec(inputs["L0_attn_a_bg"], NF),
        "abmb": chunkvec(inputs["L0_attn_a_bmb"], NF),
        "amem": a(inputs["L0_attn_a_memory"]),
        "anormT": np.ascontiguousarray(
            a(inputs["L0_attn_a_norm"]).T.reshape(NFC, 128, NH).transpose(1, 0, 2)),
        "bld": a(inputs["L0_attn_b_log_decay"]),
        "bWq": a(inputs["L0_attn_b_Wq"]), "bWk": a(inputs["L0_attn_b_Wk"]),
        "bWv": a(inputs["L0_attn_b_Wv"]), "bWo": a(inputs["L0_attn_b_Wo"]),
        "bWg1": a(inputs["L0_attn_b_Wg1"]), "bWg2": a(inputs["L0_attn_b_Wg2"]),
        "bnw": chunkvec(inputs["L0_attn_b_norm_w"], NF),
        "fgk": np.ascontiguousarray(
            a(inputs["L0_ffn_gate_kernel"]).reshape(NF, 128, NEXP).transpose(1, 0, 2)),
        "fwi0": a(inputs["L0_ffn_wi_0"]), "fwi1": a(inputs["L0_ffn_wi_1"]),
        "fwo": a(inputs["L0_ffn_wo"]),
    }
    in_maps = []
    for b in range(B):
        m = dict(common)
        m["x0"] = np.ascontiguousarray(hs[b].T)
        in_maps.append(m)
    return in_maps


def kernel(**inputs):
    in_maps = _prep_inputs(inputs)
    nc = build_program()
    res = run_bass_kernel_spmd(nc, in_maps, core_ids=list(range(B)))
    out = np.stack([np.asarray(res.results[b]["out"]).T for b in range(B)], axis=0)
    return out.astype(np.float32)
